# version 17
# speedup vs baseline: 1.1971x; 1.0407x over previous
import contextlib
import numpy as np
import ml_dtypes
import concourse.bass as bass
import concourse.mybir as mybir
from concourse.bass_utils import run_bass_kernel_spmd

F32 = mybir.dt.float32
BF16 = mybir.dt.bfloat16
AF = mybir.ActivationFunctionType
ALU = mybir.AluOpType
AX = mybir.AxisListType
BF = ml_dtypes.bfloat16

NCORES = 8
B, T, D = 16, 4096, 1024
BPC = B // NCORES
DC = D // 128
E1 = 2048
EC1 = E1 // 128
FF = 4096
FC = FF // 128
NST = 16
HEADS, LOCAL_HEADS, DH = 16, 4, 64
WINDOW, BLIND = 128, 64


def _flat_waits(wait):
    w = []
    for tok in wait:
        if not tok:
            continue
        if isinstance(tok, tuple):
            assert len(tok) == 2 and isinstance(tok[0], str), f"bad {tok!r}"
            w.append(tok)
        else:
            for p in tok:
                assert isinstance(p, tuple) and len(p) == 2, \
                    f"bad pair {p!r} in {tok!r}"
                w.append(p)
    return w


class Prog:

    ENGINES = ("sync", "scalar", "vector", "tensor", "gpsimd")
    DMA_SEM = {"sync": "dS", "scalar": "dC", "vector": "dV",
               "tensor": "dT", "gpsimd": "dG"}
    ENG_SEM = {"sync": "y", "scalar": "s", "vector": "v",
               "tensor": "t", "gpsimd": "g"}
    NSLOT = 4

    def __init__(self, nc):
        self.nc = nc
        self.ops = []
        self.cnt = {}
        self._dma_idx = {}

    def _bump(self, sem, amt):
        self.cnt[sem] = self.cnt.get(sem, 0) + amt

    def op(self, engine, meth, wait=(), sem=None, lp=False, **kwargs):
        if sem is None:
            sem = self.ENG_SEM[engine]
        w = _flat_waits(wait)

        import traceback as _tb
        site = _tb.extract_stack()[-2]

        def fn(eng, _m=meth, _kw=dict(kwargs), _lp=lp, _nc=self.nc,
               _site=site):
            try:
                if _lp:
                    with _nc.allow_low_precision(
                            reason="intentional bf16 op"):
                        return getattr(eng, _m)(**_kw)
                return getattr(eng, _m)(**_kw)
            except Exception as ex:
                raise RuntimeError(
                    f"op {_m} appended at {_site.filename}:{_site.lineno}"
                ) from ex
        self.ops.append((engine, fn, w, [(sem, 1)]))
        self._bump(sem, 1)
        return [(sem, self.cnt[sem])]

    def dma(self, queue, out, in_, wait=()):
        slot = self._dma_idx.get(queue, 0)
        self._dma_idx[queue] = (slot + 1) % self.NSLOT
        sem = self.DMA_SEM[queue] + str(slot)
        w = [(sem, self.cnt.get(sem, 0))] if self.cnt.get(sem, 0) else []
        w += _flat_waits(wait)
        self.ops.append(
            (queue, lambda e, o=out, i=in_: e.dma_start(out=o, in_=i),
             w, [(sem, 16)]))
        self._bump(sem, 16)
        return [(sem, self.cnt[sem])]

    def now(self, *sems):
        out = []
        for s in sems:
            if s in ("dS", "dC", "dV", "dT", "dG"):
                for k in range(self.NSLOT):
                    sk = s + str(k)
                    if self.cnt.get(sk, 0) > 0:
                        out.append((sk, self.cnt[sk]))
            elif self.cnt.get(s, 0) > 0:
                out.append((s, self.cnt[s]))
        return out

    def all_now(self):
        return [(s, v) for s, v in self.cnt.items() if v > 0]

    def fence(self):
        tok = self.all_now()
        for e in self.ENGINES:
            self.ops.append((e, None, list(tok), []))
        return tok

    def build(self):
        nc = self.nc
        sem_names = sorted(self.cnt.keys())
        with contextlib.ExitStack() as st:
            sems = {s: st.enter_context(nc.semaphore(s)) for s in sem_names}
            with nc.Block() as block:
                for eng_name in self.ENGINES:
                    my = [(fn, w, inc) for (e, fn, w, inc) in self.ops
                          if e == eng_name]
                    if not my:
                        continue
                    deco = getattr(block, eng_name)

                    def body(eng, _my=my, _sems=sems):
                        hw = {}
                        for fn, waits, incs in _my:
                            for s, v in waits:
                                if v > hw.get(s, 0):
                                    eng.wait_ge(_sems[s], v)
                                    hw[s] = v
                            if fn is None:
                                continue
                            inst = fn(eng)
                            for s, a in incs:
                                inst = inst.then_inc(_sems[s], a)
                    deco(body)
        return nc


CPOOL_VALS = [1e-5, 1.0, 1e-6, 1e-8, 0.5, -1.0, 2.0, 0.0]


def _setup_cpool(nc, P, en):
    cp_d = nc.dram_tensor("cpool", [128, len(CPOOL_VALS)], F32,
                          kind="ExternalInput").ap()
    cp = en(nc.sbuf_tensor("cp", [128, len(CPOOL_VALS)], F32))
    P.dma("sync", out=cp[:, :], in_=cp_d)
    for i, v in enumerate(CPOOL_VALS):
        nc.const_aps.register(F32, v, cp[:, i:i + 1])
    P.fence()
    return cp


def _ln_rows(P, rt, ps_m, ps_q, W, denom, mm_tok):
    tokm = P.op("vector", "tensor_scalar_mul", out=rt["negmu"][0:1, 0:W],
                in0=ps_m[0:1, 0:W], scalar1=-1.0 / denom, wait=[mm_tok])
    tokq = P.op("vector", "tensor_scalar_mul", out=rt["r0"][0:1, 0:W],
                in0=ps_q[0:1, 0:W], scalar1=1.0 / denom, wait=[mm_tok])
    tok2 = P.op("vector", "tensor_mul", out=rt["r1"][0:1, 0:W],
                in0=rt["negmu"][0:1, 0:W], in1=rt["negmu"][0:1, 0:W],
                wait=[tokm])
    tokv = P.op("vector", "tensor_sub", out=rt["var"][0:1, 0:W],
                in0=rt["r0"][0:1, 0:W], in1=rt["r1"][0:1, 0:W],
                wait=[tokq, tok2])
    return tokm, tokv


def _emit_quantize(nc, P, en, wq_tile, W_ap, nchunks, width, pfx,
                   rows, onesf, t_ld1, thr_bias=0.0):
    wtmp = [en(nc.sbuf_tensor(f"{pfx}wtmp{i}", [128, width], F32))
            for i in range(2)]
    accp = en(nc.sbuf_tensor(f"{pfx}acc", [128, 2], F32))
    thr = en(nc.sbuf_tensor(f"{pfx}thr", [128, 1], F32))
    sg = en(nc.sbuf_tensor(f"{pfx}sg", [128, width], BF16))
    ab = en(nc.sbuf_tensor(f"{pfx}ab", [128, width], F32))
    mk = en(nc.sbuf_tensor(f"{pfx}mk", [128, width], BF16))
    ps_a = en(nc.psum_tensor(f"{pfx}psa", [1, 2], F32))
    red = P.op("vector", "memset", ap=accp[:, 0:1], constant=0.0)
    for c in range(nchunks):
        buf = wtmp[c % 2]
        ld = P.dma("sync", out=buf[:, :], in_=W_ap[c],
                   wait=[red] if c >= 2 else [])
        red = P.op("vector", "tensor_reduce", out=accp[:, 1:2],
                   in_=buf[:, :], axis=AX.X, op=ALU.add,
                   apply_absolute_value=True, wait=[ld, red])
        red = P.op("vector", "tensor_add", out=accp[:, 0:1],
                   in0=accp[:, 0:1], in1=accp[:, 1:2], wait=[red])
    tmm = P.op("tensor", "matmul", out=ps_a[0:1, 0:1],
               lhsT=accp[:, 0:1], rhs=onesf[:, 0:1],
               start=True, stop=True, wait=[red, t_ld1])
    trow = P.op("vector", "tensor_scalar", out=accp[0:1, 0:1],
                in0=ps_a[0:1, 0:1],
                scalar1=0.5 / (nchunks * 128 * width), scalar2=thr_bias,
                op0=ALU.mult, op1=ALU.add, wait=[tmm])
    tst = P.dma("gpsimd", out=rows[63:64, 0:1], in_=accp[0:1, 0:1],
                wait=[trow])
    tbc = P.dma("sync", out=thr[:, :],
                in_=rows[63:64, 0:1].broadcast_to((128, 1)), wait=[tst])
    t4 = None
    prev = [None, None]
    for c in range(nchunks):
        buf = wtmp[c % 2]
        w = [prev[c % 2]] if prev[c % 2] else []
        ld = P.dma("sync", out=buf[:, :], in_=W_ap[c], wait=w + [tbc])
        t1 = P.op("scalar", "activation", out=sg[:, :], in_=buf[:, :],
                  func=AF.Sign, wait=[ld, t4])
        t2 = P.op("scalar", "activation", out=ab[:, :], in_=buf[:, :],
                  func=AF.Abs, wait=[ld, t4])
        t3 = P.op("vector", "tensor_single_scalar", out=mk[:, :],
                  in_=ab[:, :], scalar=thr[:, 0:1], op=ALU.is_gt,
                  wait=[t2, tbc, t4])
        t4 = P.op("vector", "tensor_mul",
                  out=wq_tile[:, c * width:(c + 1) * width],
                  in0=sg[:, :], in1=mk[:, :], wait=[t1, t3])
        prev[c % 2] = t4
    return P.now("v", "s")


def emit_phase_a(nc, P, G):
    _pfx = "A_"
    x_in = G["x"]; Wq_d = G["Wq"]; Wk_d = G["Wk"]; Wv_d = G["Wv"]
    Wo_d = G["Wo"]; bo_d = G["bo"]
    ones_bf = G["ones_bf"]; ones_f = G["ones_f"]; eye_f = G["eye_f"]
    eye_b = G["eye_bf"]; mask_d = G["mask256"]; maskf_d = G["mask256f"]
    bd2_d = G["bdiag2"]; sel2_d = G["sel2"]; triu_d = G["triu64"]
    x1 = G["x1"]
    xfm_d = G["xfm_scr"]; qs_d = G["qs_scr"]; kgT_d = G["kgT_scr"]
    vtg_d = G["vtg_scr"]; aloc_d = G["aloc_scr"]; aglob_d = G["aglob_scr"]
    rows = G["rows_scr"]
    NU = T // BLIND
    NT = T // 512
    W = 512

    with contextlib.ExitStack() as st:
        en = st.enter_context
        onesb = en(nc.sbuf_tensor(_pfx + "onesb", [128, 2], BF16))
        onesf = en(nc.sbuf_tensor(_pfx + "onesf", [128, 2], F32))
        eyef = en(nc.sbuf_tensor(_pfx + "eyef", [128, 128], F32))
        eyeb = en(nc.sbuf_tensor(_pfx + "eyeb", [128, 128], BF16))
        maskm = en(nc.sbuf_tensor(_pfx + "maskm", [128, 256], F32))
        maskf = en(nc.sbuf_tensor(_pfx + "maskf", [128, 256], F32))
        bd2 = en(nc.sbuf_tensor(_pfx + "bd2", [128, 2], BF16))
        sel2 = en(nc.sbuf_tensor(_pfx + "sel2t", [2, 128], BF16))
        triu = en(nc.sbuf_tensor(_pfx + "triut", [64, 64], F32))
        tC = []
        for dst, src in ((onesb, ones_bf), (onesf, ones_f), (eyef, eye_f),
                         (eyeb, eye_b), (maskm, mask_d), (maskf, maskf_d),
                         (bd2, bd2_d), (sel2, sel2_d), (triu, triu_d)):
            tC.append(P.dma("sync", out=dst[:, :], in_=src))
        tC = P.now("dS")

        wq_sb = en(nc.sbuf_tensor(_pfx + "wq_sb", [128, DC * D], BF16))
        wk_sb = en(nc.sbuf_tensor(_pfx + "wk_sb", [128, DC * D], BF16))
        wv_sb = en(nc.sbuf_tensor(_pfx + "wv_sb", [128, DC * D], BF16))
        with contextlib.ExitStack() as sw:
            wtmp = sw.enter_context(nc.sbuf_tensor(_pfx + "wtmpa", [128, D], F32))
            tw = None
            for w_sb, w_d in ((wq_sb, Wq_d), (wk_sb, Wk_d), (wv_sb, Wv_d)):
                for c in range(DC):
                    ld = P.dma("sync", out=wtmp[:, :],
                               in_=w_d[c * 128:(c + 1) * 128, :], wait=[tw])
                    tw = P.op("vector", "tensor_copy",
                              out=w_sb[:, c * D:(c + 1) * D],
                              in_=wtmp[:, :], wait=[ld, tw])
            P.fence()

        bo_sb = en(nc.sbuf_tensor(_pfx + "bo_sb", [128, DC], F32))
        tbo = []
        for dc in range(DC):
            tbo.append(P.dma("sync", out=bo_sb[:, dc:dc + 1],
                             in_=bo_d[dc * 128:(dc + 1) * 128]
                             .rearrange("(p o) -> p o", o=1)))

        with contextlib.ExitStack() as s1:
            e1 = s1.enter_context
            x_tm = e1(nc.sbuf_tensor(_pfx + "x_tm", [128, 4 * 1024], F32))
            x_fm = e1(nc.sbuf_tensor(_pfx + "x_fm", [128, DC * W], F32))
            x_bf = e1(nc.sbuf_tensor(_pfx + "x_bfa", [128, DC * W], BF16))
            xsq = e1(nc.sbuf_tensor(_pfx + "xsqa", [128, DC * W], BF16))
            xnf = e1(nc.sbuf_tensor(_pfx + "xnfa", [128, W], F32))
            xn = e1(nc.sbuf_tensor(_pfx + "xna", [128, DC * W], BF16))
            nmb = e1(nc.sbuf_tensor(_pfx + "nmba", [128, W], F32))
            rbb = e1(nc.sbuf_tensor(_pfx + "rbba", [128, W], F32))
            q_bf = e1(nc.sbuf_tensor(_pfx + "q_bf", [128, DC * W], BF16))
            k_bf = e1(nc.sbuf_tensor(_pfx + "k_bf", [128, 2 * 640], BF16))
            kT_sb = e1(nc.sbuf_tensor(_pfx + "kT_sb", [128, 5 * 1024], BF16))
            vT_sb = e1(nc.sbuf_tensor(_pfx + "vT_sb", [128, 5 * 1024], BF16))
            eqb = e1(nc.sbuf_tensor(_pfx + "eqb", [128, 6 * W], BF16))
            rzb = e1(nc.sbuf_tensor(_pfx + "rzb", [2, 6 * W], BF16))
            qs_c = e1(nc.sbuf_tensor(_pfx + "qs_c", [128, W], BF16))
            elu1 = e1(nc.sbuf_tensor(_pfx + "elu1", [128, 768], BF16))
            elu2 = e1(nc.sbuf_tensor(_pfx + "elu2", [128, 768], F32))
            elu3 = e1(nc.sbuf_tensor(_pfx + "elu3", [128, 768], F32))
            kg_s = e1(nc.sbuf_tensor(_pfx + "kg_s", [128, 768], BF16))
            ed = e1(nc.sbuf_tensor(_pfx + "ed", [128, 256], F32))
            em = e1(nc.sbuf_tensor(_pfx + "em", [128, 256], BF16))
            emT = e1(nc.sbuf_tensor(_pfx + "emT", [128, 256], BF16))
            att_tm = e1(nc.sbuf_tensor(_pfx + "att_tm", [128, 256], BF16))
            aloc_t = e1(nc.sbuf_tensor(_pfx + "aloc_t", [128, 2 * W], BF16))
            rden4 = e1(nc.sbuf_tensor(_pfx + "rden4", [128, 8], F32))
            rt = {k: e1(nc.sbuf_tensor(_pfx + "a1" + k, [1, 512], F32))
                  for k in ("negmu", "var", "r0", "r1")}
            ps_s2 = e1(nc.psum_tensor(_pfx + "ps_s2", [2, 512], F32))
            ps_q = e1(nc.psum_tensor(_pfx + "ps_qa", [1, 512], F32))
            ps_b = [e1(nc.psum_tensor(_pfx + f"ps_ba{i}", [128, 512], F32))
                    for i in range(2)]
            ps_dots = [e1(nc.psum_tensor(_pfx + f"ps_dots{i}", [128, 256],
                                         F32)) for i in range(2)]
            ps_trb = e1(nc.psum_tensor(_pfx + "ps_trb", [128, 256], BF16))
            ps_att = e1(nc.psum_tensor(_pfx + "ps_att", [128, 256], F32))

            prev_iter = tC
            for b in range(BPC):
                mz = [P.op("vector", "memset", ap=k_bf[:, 0:128],
                           constant=0.0, wait=prev_iter),
                      P.op("vector", "memset", ap=k_bf[:, 640:768],
                           constant=0.0),
                      P.op("vector", "memset", ap=kT_sb[:, 0:1024],
                           constant=0.0),
                      P.op("vector", "memset", ap=vT_sb[:, 0:1024],
                           constant=0.0)]
                prev_iter = prev_iter + P.now("v")
                for ti in range(NT):
                    t0 = ti * W
                    ldx = P.dma("sync",
                                out=x_tm.ap().rearrange("p (s d) -> p s d",
                                                        s=4),
                                in_=x_in[b, t0:t0 + W, :].rearrange(
                                    "(s p) d -> p s d", p=128),
                                wait=prev_iter)
                    ttr = None
                    tcp = None
                    tcps = [None, None]
                    for ii, (s4, dc) in enumerate(
                            (s4, dc) for s4 in range(4) for dc in range(DC)):
                        pb = ps_b[ii % 2]
                        ttr = P.op("tensor", "transpose",
                                   out=pb[:, 0:128],
                                   in_=x_tm[:, s4 * 1024 + dc * 128:
                                            s4 * 1024 + (dc + 1) * 128],
                                   identity=eyef[:, :],
                                   wait=[ldx, tC, tcps[ii % 2], prev_iter])
                        tcp = P.op("vector", "tensor_copy",
                                   out=x_fm[:, dc * W + s4 * 128:
                                            dc * W + (s4 + 1) * 128],
                                   in_=pb[:, 0:128], wait=[ttr])
                        tcps[ii % 2] = tcp
                    stx = P.dma("gpsimd",
                                out=xfm_d[b, :, t0:t0 + W].rearrange(
                                    "(c p) t -> p c t", p=128),
                                in_=x_fm.ap().rearrange("p (c t) -> p c t",
                                                        c=DC), wait=[tcp])
                    tcast = P.op("vector", "tensor_copy", out=x_bf[:, :],
                                 in_=x_fm[:, :], wait=[tcp] + prev_iter)
                    tsq = P.op("scalar", "activation", out=xsq[:, :],
                               in_=x_bf[:, :], func=AF.Square,
                               wait=[tcast] + prev_iter)
                    tm = None
                    for c in range(DC):
                        tm = P.op("tensor", "matmul", out=ps_s2[0:1, :],
                                  lhsT=onesb[:, 0:1],
                                  rhs=x_bf[:, c * W:(c + 1) * W],
                                  start=(c == 0), stop=(c == DC - 1),
                                  wait=[tcast] + prev_iter if c == 0 else [])
                    for c in range(DC):
                        tm = P.op("tensor", "matmul", out=ps_q[0:1, :],
                                  lhsT=onesb[:, 0:1],
                                  rhs=xsq[:, c * W:(c + 1) * W],
                                  start=(c == 0), stop=(c == DC - 1),
                                  wait=[tsq] if c == 0 else [])
                    tokm, tokv = _ln_rows(P, rt, ps_s2, ps_q, W, D, tm)
                    tl1 = P.op("scalar", "activation", out=rt["r0"][0:1, :],
                               in_=rt["var"][0:1, :], func=AF.Ln, bias=1e-5,
                               wait=[tokv])
                    tr1 = P.op("scalar", "activation", out=rt["r1"][0:1, :],
                               in_=rt["r0"][0:1, :], func=AF.Exp, scale=-0.5,
                               wait=[tl1])
                    s0 = P.dma("gpsimd", out=rows[0:1, :],
                               in_=rt["negmu"][0:1, :], wait=[tokm])
                    s1r = P.dma("gpsimd", out=rows[1:2, :],
                                in_=rt["r1"][0:1, :], wait=[tr1])
                    b0 = P.dma("sync", out=nmb[:, :],
                               in_=rows[0:1, :].broadcast_to((128, W)),
                               wait=[s0])
                    b1t = P.dma("sync", out=rbb[:, :],
                               in_=rows[1:2, :].broadcast_to((128, W)),
                               wait=[s1r])
                    tn = None
                    for c in range(DC):
                        t_ = P.op("vector", "tensor_add",
                                  out=xnf[:, :],
                                  in0=x_fm[:, c * W:(c + 1) * W],
                                  in1=nmb[:, :],
                                  wait=([b0, tcp] + prev_iter if c == 0
                                        else []) + [tn])
                        tn = P.op("vector", "tensor_mul",
                                  out=xn[:, c * W:(c + 1) * W],
                                  in0=xnf[:, :],
                                  in1=rbb[:, :],
                                  wait=[b1t, t_] if c == 0 else [t_])
                    pp_free = [tcps[0] + tn, tcps[1] + tn]
                    for ec in range(DC):
                        ps = ps_b[ec % 2]
                        tmm = None
                        for c in range(DC):
                            tmm = P.op(
                                "tensor", "matmul", out=ps[:, :],
                                lhsT=wq_sb[:, c * D + ec * 128:
                                           c * D + ec * 128 + 128],
                                rhs=xn[:, c * W:(c + 1) * W],
                                start=(c == 0), stop=(c == DC - 1),
                                wait=[tn, pp_free[ec % 2]] if c == 0 else [])
                        ev = P.op("scalar", "copy",
                                  out=q_bf[:, ec * W:(ec + 1) * W],
                                  in_=ps[:, :], wait=[tmm])
                        pp_free[ec % 2] = ev
                    for ec in range(2):
                        ps = ps_b[ec % 2]
                        tmm = None
                        for c in range(DC):
                            tmm = P.op(
                                "tensor", "matmul", out=ps[:, :],
                                lhsT=wk_sb[:, c * D + ec * 128:
                                           c * D + ec * 128 + 128],
                                rhs=xn[:, c * W:(c + 1) * W],
                                start=(c == 0), stop=(c == DC - 1),
                                wait=[pp_free[ec % 2]] if c == 0 else [])
                        ev = P.op("scalar", "copy",
                                  out=k_bf[:, ec * 640 + 128:
                                           ec * 640 + 128 + W],
                                  in_=ps[:, :], wait=[tmm])
                        pp_free[ec % 2] = ev
                    kv_done = []
                    for s4 in range(4):
                        for (wmat, dstT) in ((wk_sb, kT_sb), (wv_sb, vT_sb)):
                            for hf in range(2):
                                ps = ps_b[hf]
                                tmm = None
                                for c in range(DC):
                                    tmm = P.op(
                                        "tensor", "matmul", out=ps[:, :],
                                        lhsT=xn[:, c * W + s4 * 128:
                                                c * W + (s4 + 1) * 128],
                                        rhs=wmat[:, c * D + hf * W:
                                                 c * D + (hf + 1) * W],
                                        start=(c == 0), stop=(c == DC - 1),
                                        wait=[tn, pp_free[hf]]
                                        if c == 0 else [])
                                ev = P.op(
                                    "scalar", "copy",
                                    out=dstT[:, (1 + s4) * 1024 + hf * W:
                                             (1 + s4) * 1024 +
                                             (hf + 1) * W],
                                    in_=ps[:, :], wait=[tmm])
                                pp_free[hf] = ev
                                kv_done.append(ev)
                    ekg = []
                    for s4 in range(4):
                        base = (1 + s4) * 1024 + 256
                        t1 = P.op("vector", "tensor_scalar_min",
                                  out=elu1[:, :],
                                  in0=kT_sb[:, base:base + 768],
                                  scalar1=0.0,
                                  wait=kv_done + ekg)
                        t2 = P.op("scalar", "activation", out=elu2[:, :],
                                  in_=elu1[:, :], func=AF.Exp, wait=[t1])
                        t3 = P.op("vector", "tensor_relu", out=elu3[:, :],
                                  in_=kT_sb[:, base:base + 768],
                                  wait=[t1])
                        t4 = P.op("vector", "tensor_add", out=kg_s[:, :],
                                  in0=elu2[:, :], in1=elu3[:, :],
                                  wait=[t2, t3])
                        e1_ = P.dma("gpsimd",
                                    out=kgT_d[b, t0 + s4 * 128:
                                              t0 + (s4 + 1) * 128, :],
                                    in_=kg_s[:, :], wait=[t4])
                        e2_ = P.dma("gpsimd",
                                    out=vtg_d[b, t0 + s4 * 128:
                                              t0 + (s4 + 1) * 128, :],
                                    in_=vT_sb[:, base:base + 768],
                                    wait=kv_done)
                        ekg = [t4, e1_, e2_]
                    teq = P.op("scalar", "activation", out=eqb[:, :],
                               in_=q_bf[:, 2 * W:8 * W], func=AF.Exp,
                               wait=P.now("s"))
                    tqs = []
                    for ci in range(6):
                        tz1 = P.op("tensor", "matmul", out=ps_s2[0:2, :],
                                   lhsT=bd2[:, :],
                                   rhs=eqb[:, ci * W:(ci + 1) * W],
                                   start=True, stop=True,
                                   wait=[teq, tm, tokm, tokv] + tqs)
                        tz2 = P.op("vector", "reciprocal",
                                   out=rzb[:, ci * W:(ci + 1) * W],
                                   in_=ps_s2[0:2, :], lp=True, wait=[tz1])
                        tz3 = P.op("tensor", "matmul", out=ps_b[0][:, :],
                                   lhsT=sel2[:, :],
                                   rhs=rzb[:, ci * W:(ci + 1) * W],
                                   start=True, stop=True,
                                   wait=[tz2, pp_free[0]])
                        tz4 = P.op("vector", "tensor_mul", out=qs_c[:, :],
                                   in0=eqb[:, ci * W:(ci + 1) * W],
                                   in1=ps_b[0][:, :], wait=[tz3] + tqs)
                        tz5 = P.dma("gpsimd",
                                    out=qs_d[b, ci * 128:(ci + 1) * 128,
                                             t0:t0 + W],
                                    in_=qs_c[:, :], wait=[tz4])
                        pp_free[0] = tz4
                        tqs = [tz4, tz5]
                    tloc = []
                    bkt_evs = []
                    bkt_tes = [None, None]
                    for ub in range(4):
                        first = (ti == 0 and ub == 0)
                        msk = maskf if first else maskm
                        trcs = []
                        last_ta2 = None
                        tmk = None
                        tes = dict(enumerate(bkt_tes))
                        for h in range(LOCAL_HEADS):
                            ch = h // 2
                            r0 = 64 * (h % 2)
                            td = P.op(
                                "tensor", "matmul",
                                out=ps_dots[h % 2][:, :],
                                lhsT=q_bf[r0:r0 + 64,
                                          ch * W + ub * 128:
                                          ch * W + (ub + 1) * 128],
                                rhs=k_bf[r0:r0 + 64,
                                         ch * 640 + ub * 128:
                                         ch * 640 + ub * 128 + 256],
                                start=True, stop=True,
                                wait=[tn, tes[h % 2]] + tloc)
                            te = P.op("scalar", "activation", out=ed[:, :],
                                      in_=ps_dots[h % 2][:, :],
                                      func=AF.Exp,
                                      scale=0.125, wait=[td, tmk])
                            tes[h % 2] = te
                            tmk = P.op("vector", "tensor_mul", out=em[:, :],
                                       in0=ed[:, :], in1=msk[:, :],
                                       wait=[te, tC])
                            tsum = P.op("vector", "tensor_reduce",
                                        out=rden4[:, h:h + 1],
                                        in_=em[:, :], axis=AX.X, op=ALU.add,
                                        wait=[tmk])
                            trcs.append(P.op("vector", "reciprocal",
                                             out=rden4[:, 4 + h:5 + h],
                                             in_=rden4[:, h:h + 1],
                                             wait=[tsum]))
                            tt1 = P.op("tensor", "transpose",
                                       out=ps_trb[:, 0:128],
                                       in_=em[:, 0:128], identity=eyeb[:, :],
                                       lp=True, wait=[tmk, te])
                            tt2 = P.op("tensor", "transpose",
                                       out=ps_trb[:, 128:256],
                                       in_=em[:, 128:256],
                                       identity=eyeb[:, :], lp=True,
                                       wait=[tmk])
                            tc1 = P.op("vector", "tensor_copy",
                                       out=emT[:, 0:128],
                                       in_=ps_trb[:, 0:128], wait=[tt1])
                            tc2 = P.op("vector", "tensor_copy",
                                       out=emT[:, 128:256],
                                       in_=ps_trb[:, 128:256], wait=[tt2])
                            ta1 = P.op("tensor", "matmul",
                                       out=ps_att[:, 64 * h:64 * h + 64],
                                       lhsT=emT[:, 0:128],
                                       rhs=vT_sb[:, ub * 1024 + 64 * h:
                                                 ub * 1024 + 64 * h + 64],
                                       start=True, stop=False,
                                       wait=[tc1, tc2] + kv_done + bkt_evs)
                            last_ta2 = P.op(
                                "tensor", "matmul",
                                out=ps_att[:, 64 * h:64 * h + 64],
                                lhsT=emT[:, 128:256],
                                rhs=vT_sb[:, (ub + 1) * 1024 + 64 * h:
                                          (ub + 1) * 1024 + 64 * h + 64],
                                start=False, stop=True, wait=[ta1])
                        tevh = []
                        for h in range(LOCAL_HEADS):
                            tevh.append(P.op(
                                "vector", "tensor_scalar_mul",
                                out=att_tm[:, 64 * h:64 * h + 64],
                                in0=ps_att[:, 64 * h:64 * h + 64],
                                scalar1=rden4[:, 4 + h:5 + h],
                                wait=[last_ta2, trcs[h]]))
                        bkt_evs = list(tevh)
                        bkt_tes = [tes.get(0), tes.get(1)]
                        tl2 = []
                        for chl in range(2):
                            tt = P.op("tensor", "transpose",
                                      out=ps_trb[:, chl * 128:
                                                 (chl + 1) * 128],
                                      in_=att_tm[:, chl * 128:
                                                 (chl + 1) * 128],
                                      identity=eyeb[:, :], lp=True,
                                      wait=tevh + tloc)
                            tc = P.op("vector", "tensor_copy",
                                      out=aloc_t[:, chl * W + ub * 128:
                                                 chl * W + (ub + 1) * 128],
                                      in_=ps_trb[:, chl * 128:
                                                 (chl + 1) * 128],
                                      wait=[tt])
                            tl2.append(tc)
                        tloc = tl2
                    tst = P.dma("gpsimd",
                                out=aloc_d[b, :, t0:t0 + W].rearrange(
                                    "(c p) t -> p c t", p=128),
                                in_=aloc_t.ap().rearrange("p (c t) -> p c t",
                                                          c=2),
                                wait=tloc)
                    cr = [P.op("vector", "tensor_copy", out=k_bf[:, 0:128],
                               in_=k_bf[:, 512:640], wait=tloc + P.now("t")),
                          P.op("vector", "tensor_copy",
                               out=k_bf[:, 640:768],
                               in_=k_bf[:, 1152:1280], wait=[]),
                          P.op("vector", "tensor_copy",
                               out=vT_sb[:, 0:1024],
                               in_=vT_sb[:, 4096:5120], wait=[]),
                          P.op("vector", "tensor_copy",
                               out=kT_sb[:, 0:1024],
                               in_=kT_sb[:, 4096:5120], wait=[])]
                    prev_iter = (P.now("v", "s", "t")
                                 + P.now("dG", "dS"))
            P.fence()

        with contextlib.ExitStack() as s2:
            e2 = s2.enter_context
            wo_sb = e2(nc.sbuf_tensor(_pfx + "wo_sb", [128, DC * D], BF16))
            with contextlib.ExitStack() as sw:
                wtmp = sw.enter_context(nc.sbuf_tensor(_pfx + "wtmpo", [128, D],
                                                       F32))
                tw = None
                for c in range(DC):
                    ld = P.dma("sync", out=wtmp[:, :],
                               in_=Wo_d[c * 128:(c + 1) * 128, :], wait=[tw])
                    tw = P.op("vector", "tensor_copy",
                              out=wo_sb[:, c * D:(c + 1) * D],
                              in_=wtmp[:, :], wait=[ld, tw])
                P.fence()
            kgt_t = e2(nc.sbuf_tensor(_pfx + "kgt_t", [128, 768], BF16))
            bkp = e2(nc.sbuf_tensor(_pfx + "bkp", [2, 768], F32))
            bk_sum = e2(nc.sbuf_tensor(_pfx + "bk_sum", [NU, 768], F32))
            rnT = e2(nc.sbuf_tensor(_pfx + "rnT", [128, 6 * NU], F32))
            NG = T // 128
            qs_h = [e2(nc.sbuf_tensor(_pfx + f"qs_h{i}", [64, T], BF16))
                    for i in range(4)]
            qsn = [e2(nc.sbuf_tensor(_pfx + f"qsn{i}", [64, T], BF16))
                   for i in range(4)]
            kg_h = [e2(nc.sbuf_tensor(_pfx + f"kg_h{i}", [128, NG * 64],
                                      BF16)) for i in range(4)]
            vt_h = [e2(nc.sbuf_tensor(_pfx + f"vt_h{i}", [128, NG * 64],
                                      BF16)) for i in range(4)]
            rnp = [e2(nc.sbuf_tensor(_pfx + f"rnp{i}", [64, NU + 1], F32))
                   for i in range(4)]
            ctxn = [[e2(nc.sbuf_tensor(_pfx + f"ctxn{j}_{i}", [64, 64],
                                       BF16)) for i in range(2)]
                    for j in range(4)]
            ag = [e2(nc.sbuf_tensor(_pfx + f"ag{i}", [64, 512], BF16))
                  for i in range(4)]
            att_t = e2(nc.sbuf_tensor(_pfx + "att_t", [128, DC * W], BF16))
            xr_t = e2(nc.sbuf_tensor(_pfx + "xr_t", [128, DC * W], F32))
            x1o = [e2(nc.sbuf_tensor(_pfx + f"x1o{i}", [128, W], F32))
                   for i in range(2)]
            for b in range(BPC):
                stb = contextlib.ExitStack()
                ps_bk = stb.enter_context(
                    nc.psum_tensor(_pfx + f"ps_bk{b}", [2, 384], F32))
                ps_ct = stb.enter_context(
                    nc.psum_tensor(_pfx + f"ps_ct{b}", [128, NU], F32))
                tprev = []
                for g in range(T // 128):
                    ldk = P.dma("sync", out=kgt_t[:, :],
                                in_=kgT_d[b, g * 128:(g + 1) * 128, :],
                                wait=tprev)
                    h1 = P.op("tensor", "matmul", out=ps_bk[:, :],
                              lhsT=bd2[:, :], rhs=kgt_t[:, 0:384],
                              start=True, stop=True, wait=[ldk] + tprev)
                    c1 = P.op("vector", "tensor_copy", out=bkp[:, 0:384],
                              in_=ps_bk[:, :], wait=[h1] + tprev)
                    h2 = P.op("tensor", "matmul", out=ps_bk[:, :],
                              lhsT=bd2[:, :], rhs=kgt_t[:, 384:768],
                              start=True, stop=True, wait=[ldk, c1])
                    c2 = P.op("vector", "tensor_copy", out=bkp[:, 384:768],
                              in_=ps_bk[:, :], wait=[h2])
                    sb = P.dma("scalar", out=bk_sum[2 * g:2 * g + 2, :],
                               in_=bkp[:, :], wait=[c1, c2])
                    tprev = [c1, c2, sb]
                trn = []
                for s6 in range(6):
                    tcs = P.op("tensor", "matmul", out=ps_ct[:, :],
                               lhsT=bk_sum[:, s6 * 128:(s6 + 1) * 128],
                               rhs=triu[0:NU, 0:NU], start=True, stop=True,
                               wait=tprev + [tC] + trn)
                    ta = P.op("vector", "tensor_scalar_add",
                              out=rnT[:, s6 * NU:(s6 + 1) * NU],
                              in0=ps_ct[:, :], scalar1=1e-6, wait=[tcs])
                    trn = [P.op("vector", "reciprocal",
                                out=rnT[:, s6 * NU:(s6 + 1) * NU],
                                in_=rnT[:, s6 * NU:(s6 + 1) * NU],
                                wait=[ta])]
                P.fence()
                stb.close()
                sth = contextlib.ExitStack()
                ps_ctx = [sth.enter_context(nc.psum_tensor(
                    _pfx + f"ps_ctx{b}_{i}", [64, 64], F32))
                    for i in range(4)]
                ps_p2 = [sth.enter_context(nc.psum_tensor(
                    _pfx + f"ps_p2{b}_{i}", [64, 512], F32))
                    for i in range(4)]
                quad_done = trn
                ag_st = [None] * 4
                for j in range(3):
                    ldkv = [None] * 4
                    tqns = [None] * 4
                    zcs = [None] * 4
                    for hp in range(4):
                        h = 4 * j + hp
                        s6 = h // 2
                        r0 = 64 * (h % 2)
                        ldq = P.dma("sync", out=qs_h[hp][:, :],
                                    in_=qs_d[b, h * 64:(h + 1) * 64, :],
                                    wait=quad_done)
                        ldk = P.dma("sync",
                                    out=kg_h[hp].ap().rearrange(
                                        "p (g e) -> p g e", g=NG),
                                    in_=kgT_d[b, :, h * 64:(h + 1) * 64]
                                    .rearrange("(g p) e -> p g e", p=128),
                                    wait=quad_done)
                        ldv = P.dma("sync",
                                    out=vt_h[hp].ap().rearrange(
                                        "p (g e) -> p g e", g=NG),
                                    in_=vtg_d[b, :, h * 64:(h + 1) * 64]
                                    .rearrange("(g p) e -> p g e", p=128),
                                    wait=quad_done)
                        ldkv[hp] = ldk + ldv
                        z1 = P.op("vector", "memset", ap=rnp[hp][:, 0:1],
                                  constant=0.0, wait=quad_done)
                        z2 = P.op("vector", "tensor_copy",
                                  out=rnp[hp][:, 1:NU + 1],
                                  in_=rnT[r0:r0 + 64,
                                          s6 * NU:(s6 + 1) * NU],
                                  wait=trn + [z1])
                        tqns[hp] = P.op(
                            "vector", "tensor_mul",
                            out=qsn[hp].ap().rearrange("p (u r) -> p u r",
                                                       u=NU),
                            in0=qs_h[hp].ap().rearrange("p (u r) -> p u r",
                                                        u=NU),
                            in1=rnp[hp][:, 0:NU]
                            .rearrange("p (u o) -> p u o", o=1)
                            .broadcast_to((64, NU, 64)),
                            wait=[ldq, z2] + quad_done)
                        zcs[hp] = P.op("vector", "memset",
                                       ap=ctxn[hp][1][:, :],
                                       constant=0.0, wait=quad_done)
                    ev_tok = [zcs[0], zcs[1], zcs[2], zcs[3]]
                    p2_toks = [[], [], [], []]
                    ag_done = [None] * 4
                    for u in range(NU):
                        lr = 64 * (u % 2)
                        gcol = (u // 2) * 64
                        tcxs = [None] * 4
                        tp2s = [None] * 4
                        for hp in range(4):
                            tcxs[hp] = P.op(
                                "tensor", "matmul", out=ps_ctx[hp][:, :],
                                lhsT=kg_h[hp][lr:lr + 64, gcol:gcol + 64],
                                rhs=vt_h[hp][lr:lr + 64, gcol:gcol + 64],
                                start=(u == 0), stop=True,
                                skip_group_check=True,
                                wait=[ldkv[hp], ev_tok[hp]])
                            tp2s[hp] = P.op(
                                "tensor", "matmul",
                                out=ps_p2[hp][:, (u % 8) * 64:
                                              (u % 8) * 64 + 64],
                                lhsT=ctxn[hp][(u + 1) % 2][:, :],
                                rhs=qsn[hp][:, u * 64:(u + 1) * 64],
                                start=True, stop=True,
                                wait=[tqns[hp], zcs[hp], ev_tok[hp],
                                      ag_done[hp]])
                        for hp in range(4):
                            ev_tok[hp] = P.op(
                                "scalar", "copy", out=ctxn[hp][u % 2][:, :],
                                in_=ps_ctx[hp][:, :],
                                wait=[tcxs[hp], tp2s[hp]])
                            p2_toks[hp].append(tp2s[hp])
                        if u % 8 == 7:
                            for hp in range(4):
                                h = 4 * j + hp
                                cg = P.op("scalar", "copy",
                                          out=ag[hp][:, :],
                                          in_=ps_p2[hp][:, :],
                                          wait=p2_toks[hp] +
                                          ([ag_st[hp]] if ag_st[hp]
                                           else []))
                                ag_st[hp] = P.dma(
                                    "gpsimd",
                                    out=aglob_d[b, h * 64:(h + 1) * 64,
                                                (u - 7) * 64:(u + 1) * 64],
                                    in_=ag[hp][:, :], wait=[cg])
                                ag_done[hp] = cg
                                p2_toks[hp] = []
                    quad_done = P.now("v", "s", "t") + P.now("dG", "dS")
                gprev = quad_done
                P.fence()
                sth.close()
                sto = contextlib.ExitStack()
                ps_o = [sto.enter_context(nc.psum_tensor(
                    _pfx + f"ps_oa{b}_{i}", [128, 512], F32))
                    for i in range(2)]
                oprev = gprev
                for ti in range(NT):
                    t0 = ti * W
                    lda = P.dma("sync",
                                out=att_t.ap()
                                .rearrange("p (c t) -> p c t", c=DC)[:, 0:2,
                                                                    :],
                                in_=aloc_d[b, :, t0:t0 + W].rearrange(
                                    "(c p) t -> p c t", p=128), wait=oprev)
                    lda2 = P.dma("sync",
                                 out=att_t.ap()
                                 .rearrange("p (c t) -> p c t",
                                            c=DC)[:, 2:8, :],
                                 in_=aglob_d[b, :, t0:t0 + W].rearrange(
                                     "(c p) t -> p c t", p=128), wait=oprev)
                    ldr = P.dma("sync",
                                out=xr_t.ap().rearrange("p (c t) -> p c t",
                                                        c=DC),
                                in_=xfm_d[b, :, t0:t0 + W].rearrange(
                                    "(c p) t -> p c t", p=128), wait=oprev)
                    evs = []
                    x1_st = [None, None]
                    for dc in range(DC):
                        ps = ps_o[dc % 2]
                        tmm = None
                        for ec in range(DC):
                            tmm = P.op(
                                "tensor", "matmul", out=ps[:, :],
                                lhsT=wo_sb[:, ec * D + dc * 128:
                                           ec * D + dc * 128 + 128],
                                rhs=att_t[:, ec * W:(ec + 1) * W],
                                start=(ec == 0), stop=(ec == DC - 1),
                                wait=[lda, lda2] +
                                ([evs[-2]] if dc >= 2 else oprev)
                                if ec == 0 else [])
                        ev = P.op(
                            "vector", "scalar_tensor_tensor",
                            out=x1o[dc % 2][:, :],
                            in0=ps[:, :], scalar=bo_sb[:, dc:dc + 1],
                            in1=xr_t[:, dc * W:(dc + 1) * W],
                            op0=ALU.add, op1=ALU.add,
                            wait=[tmm, ldr, tbo[dc]] +
                            ([x1_st[dc % 2]] if x1_st[dc % 2] else []))
                        evs.append(ev)
                        x1_st[dc % 2] = P.dma(
                            "gpsimd",
                            out=x1[b, dc * 128:(dc + 1) * 128, t0:t0 + W],
                            in_=x1o[dc % 2][:, :], wait=[ev])
                    oprev = P.now("v", "t") + P.now("dG", "dS")
                P.fence()
                sto.close()
            P.fence()


def emit_phase_d(nc, P, G):
    _pfx = "D_"
    x3 = G["x3"]; w1T = G["l1f_W1T"]; w2T = G["l1f_W2T"]
    b1d = G["l1f_b1"]; b2d = G["l1f_b2"]
    ones_bf = G["ones_bf"]; ones_f = G["ones_f"]; eye_f = G["eye_f"]
    y = G["y"]; h_dram = G["h_scr"]; rows = G["rows_scr"]
    NT = T // 512
    W = 512

    with contextlib.ExitStack() as st:
        en = st.enter_context
        onesb = en(nc.sbuf_tensor(_pfx + "onesb", [128, 2], BF16))
        onesf = en(nc.sbuf_tensor(_pfx + "onesf", [128, 2], F32))
        eyef = en(nc.sbuf_tensor(_pfx + "eyef", [128, 128], F32))
        t_ld0 = P.dma("sync", out=onesb[:, :], in_=ones_bf)
        t_ld1 = P.dma("sync", out=onesf[:, :], in_=ones_f)
        t_ld2 = P.dma("sync", out=eyef[:, :], in_=eye_f)

        with contextlib.ExitStack() as s1:
            e1 = s1.enter_context
            wq1 = e1(nc.sbuf_tensor(_pfx + "wq1", [128, DC * FF], BF16))
            w1_chunks = [w1T[c * 128:(c + 1) * 128, :] for c in range(DC)]
            with contextlib.ExitStack() as sq:
                tq = _emit_quantize(nc, P, sq.enter_context, wq1, w1_chunks,
                                    DC, FF, "qd1", rows, onesf, t_ld1,
                                    thr_bias=0.5e-5)
                P.fence()

            b1sb = e1(nc.sbuf_tensor(_pfx + "b1sb", [128, FC], F32))
            tb1 = []
            for fc in range(FC):
                tb1.append(P.dma("sync", out=b1sb[:, fc:fc + 1],
                                 in_=b1d[fc * 128:(fc + 1) * 128]
                                 .rearrange("(p o) -> p o", o=1)))
            x3t = [e1(nc.sbuf_tensor(_pfx + f"x3t{i}", [128, DC * 512],
                                     F32)) for i in range(2)]
            x3bf = [e1(nc.sbuf_tensor(_pfx + f"x3bf{i}", [128, DC * 512],
                                      BF16)) for i in range(2)]
            xsq = e1(nc.sbuf_tensor(_pfx + "xsqd", [128, DC * 512], BF16))
            xnf = e1(nc.sbuf_tensor(_pfx + "xnfd", [128, 512], F32))
            xn = [e1(nc.sbuf_tensor(_pfx + f"xnd{i}", [128, DC * 512],
                                    BF16)) for i in range(2)]
            nmb = [e1(nc.sbuf_tensor(_pfx + f"nmbd{i}", [128, 512], F32))
                   for i in range(2)]
            rbb = [e1(nc.sbuf_tensor(_pfx + f"rbbd{i}", [128, 512], F32))
                   for i in range(2)]
            hbf = [e1(nc.sbuf_tensor(_pfx + f"hbfd{i}", [128, 512], BF16))
                   for i in range(2)]
            rt2 = [{k: e1(nc.sbuf_tensor(_pfx + f"d1{k}{i}", [1, 512], F32))
                    for k in ("negmu", "var", "r0", "r1")}
                   for i in range(2)]
            ps_m = [e1(nc.psum_tensor(_pfx + f"ps_md{i}", [1, 512], F32))
                    for i in range(2)]
            ps_q = [e1(nc.psum_tensor(_pfx + f"ps_qd{i}", [1, 512], F32))
                    for i in range(2)]
            ps_big = [e1(nc.psum_tensor(_pfx + f"psbd{i}", [128, 512], F32))
                      for i in range(2)]

            tdone = [[tq], [tq]]
            tdone = [tq, tq]
            ps_free = [tq, tq]
            hb_free = [tq, tq]
            tm = None
            tn = None
            for b in range(BPC):
                for ti in range(NT):
                    it = b * NT + ti
                    p = it % 2
                    prev2 = tdone[p]
                    rt = rt2[p]
                    src = x3[b, :, ti * W:(ti + 1) * W].rearrange(
                        "(c p) t -> p c t", p=128)
                    ldx = P.dma("sync",
                                out=x3t[p].ap().rearrange(
                                    "p (c t) -> p c t", c=DC),
                                in_=src, wait=prev2)
                    tcast = P.op("vector", "tensor_copy", out=x3bf[p][:, :],
                                 in_=x3t[p][:, :], wait=[ldx] + prev2)
                    tsq = P.op("scalar", "activation", out=xsq[:, :],
                               in_=x3bf[p][:, :], func=AF.Square,
                               wait=[tcast, tm] if it else [tcast])
                    tm0 = None
                    for c in range(DC):
                        tm0 = P.op("tensor", "matmul", out=ps_m[p][0:1, :],
                                   lhsT=onesb[:, 0:1],
                                   rhs=x3bf[p][:, c * W:(c + 1) * W],
                                   start=(c == 0), stop=(c == DC - 1),
                                   wait=[tcast, t_ld0] + prev2
                                   if c == 0 else [])
                    tm = None
                    for c in range(DC):
                        tm = P.op("tensor", "matmul", out=ps_q[p][0:1, :],
                                  lhsT=onesb[:, 0:1],
                                  rhs=xsq[:, c * W:(c + 1) * W],
                                  start=(c == 0), stop=(c == DC - 1),
                                  wait=[tsq] + prev2 if c == 0 else [])
                    tokm, tokv = _ln_rows(P, rt, ps_m[p], ps_q[p], W, D,
                                          tm0 + tm)
                    tl1 = P.op("scalar", "activation", out=rt["r0"][0:1, :],
                               in_=rt["var"][0:1, :], func=AF.Ln, bias=1e-5,
                               wait=[tokv])
                    tr1 = P.op("scalar", "activation", out=rt["r1"][0:1, :],
                               in_=rt["r0"][0:1, :], func=AF.Exp, scale=-0.5,
                               wait=[tl1])
                    s0 = P.dma("gpsimd", out=rows[8 + 2 * p:9 + 2 * p, :],
                               in_=rt["negmu"][0:1, :], wait=[tokm] + prev2)
                    s1r = P.dma("gpsimd", out=rows[9 + 2 * p:10 + 2 * p, :],
                                in_=rt["r1"][0:1, :], wait=[tr1])
                    b0 = P.dma("sync", out=nmb[p][:, :],
                               in_=rows[8 + 2 * p:9 + 2 * p, :]
                               .broadcast_to((128, W)), wait=[s0])
                    b1tok = P.dma("sync", out=rbb[p][:, :],
                                  in_=rows[9 + 2 * p:10 + 2 * p, :]
                                  .broadcast_to((128, W)), wait=[s1r])
                    for c in range(DC):
                        t_ = P.op("vector", "tensor_add", out=xnf[:, :],
                                  in0=x3t[p][:, c * W:(c + 1) * W],
                                  in1=nmb[p][:, :],
                                  wait=([b0, ldx] + prev2 if c == 0
                                        else []) + [tn])
                        tn = P.op("vector", "tensor_mul",
                                  out=xn[p][:, c * W:(c + 1) * W],
                                  in0=xnf[:, :], in1=rbb[p][:, :],
                                  wait=[b1tok, t_] if c == 0 else [t_])
                    for fc in range(FC):
                        ps = ps_big[fc % 2]
                        tmm = None
                        for c in range(DC):
                            tmm = P.op(
                                "tensor", "matmul", out=ps[:, :],
                                lhsT=wq1[:, c * FF + fc * 128:
                                         c * FF + fc * 128 + 128],
                                rhs=xn[p][:, c * W:(c + 1) * W],
                                start=(c == 0), stop=(c == DC - 1),
                                wait=[tn, ps_free[fc % 2]] if c == 0 else [])
                        hb = hbf[fc % 2]
                        tg = P.op("scalar", "activation", out=hb[:, :],
                                  in_=ps[:, :], func=AF.Relu,
                                  bias=b1sb[:, fc:fc + 1],
                                  wait=[tmm, hb_free[fc % 2], tb1[fc]])
                        ps_free[fc % 2] = tg
                        sh = P.dma("gpsimd",
                                   out=h_dram[b, fc * 128:(fc + 1) * 128,
                                              ti * W:(ti + 1) * W],
                                   in_=hb[:, :], wait=[tg])
                        hb_free[fc % 2] = sh
                    tdone[p] = P.now("v", "s", "t") + P.now("dG", "dS")
            P.fence()

        with contextlib.ExitStack() as s2:
            e2 = s2.enter_context
            wq2 = e2(nc.sbuf_tensor(_pfx + "wq2", [128, FC * D], BF16))
            w2_chunks = [w2T[c * 128:(c + 1) * 128, :] for c in range(FC)]
            with contextlib.ExitStack() as sq:
                tq = _emit_quantize(nc, P, sq.enter_context, wq2, w2_chunks,
                                    FC, D, "qd2", rows, onesf, t_ld1,
                                    thr_bias=0.5e-5)
                P.fence()

            b2sb = e2(nc.sbuf_tensor(_pfx + "b2sb", [128, DC], F32))
            tb2 = []
            for dc in range(DC):
                tb2.append(P.dma("sync", out=b2sb[:, dc:dc + 1],
                                 in_=b2d[dc * 128:(dc + 1) * 128]
                                 .rearrange("(p o) -> p o", o=1)))
            x3t2 = [e2(nc.sbuf_tensor(_pfx + f"x3t2{i}", [128, DC * 512],
                                      F32)) for i in range(2)]
            x4t = [e2(nc.sbuf_tensor(_pfx + f"x4t{i}", [128, DC * 512],
                                     F32)) for i in range(2)]
            otm = [e2(nc.sbuf_tensor(_pfx + f"otm{i}", [128, 4 * 1024],
                                     F32)) for i in range(2)]
            hb2 = [e2(nc.sbuf_tensor(_pfx + f"hb2d{i}", [128, 512], BF16))
                   for i in range(2)]
            ps = [e2(nc.psum_tensor(_pfx + f"ps2d{i}", [128, 512], F32))
                  for i in range(8)]

            tdone = [tq, tq]
            evtoks = [tq] * DC
            trtok = tq
            mm_par = [tq, tq]
            mm = None
            for b in range(BPC):
                for ti in range(NT):
                    it = b * NT + ti
                    p = it % 2
                    prev2 = tdone[p]
                    src = x3[b, :, ti * W:(ti + 1) * W].rearrange(
                        "(c p) t -> p c t", p=128)
                    ldx = P.dma("sync",
                                out=x3t2[p].ap().rearrange(
                                    "p (c t) -> p c t", c=DC),
                                in_=src, wait=prev2)
                    for fc in range(FC):
                        hbuf = hb2[fc % 2]
                        ldh = P.dma(
                            "sync", out=hbuf[:, :],
                            in_=h_dram[b, fc * 128:(fc + 1) * 128,
                                       ti * W:(ti + 1) * W],
                            wait=[mm_par[fc % 2]])
                        for dc in range(DC):
                            mm = P.op(
                                "tensor", "matmul", out=ps[dc][:, :],
                                lhsT=wq2[:, fc * D + dc * 128:
                                         fc * D + dc * 128 + 128],
                                rhs=hbuf[:, :],
                                start=(fc == 0), stop=(fc == FC - 1),
                                wait=([ldh] + ([evtoks[dc], trtok]
                                               if fc == 0 else []))
                                if dc == 0 or fc == 0 else [])
                            if dc == DC - 1:
                                mm_par[fc % 2] = mm
                    evs = []
                    for dc in range(DC):
                        evs.append(P.op(
                            "vector", "scalar_tensor_tensor",
                            out=x4t[p][:, dc * W:(dc + 1) * W],
                            in0=ps[dc][:, :], scalar=b2sb[:, dc:dc + 1],
                            in1=x3t2[p][:, dc * W:(dc + 1) * W],
                            op0=ALU.add, op1=ALU.add,
                            wait=([mm, ldx, tb2[dc]] + prev2)
                            if dc == 0 else [evs[-1]]))
                        evtoks[dc] = evs[-1]
                    tev = None
                    for s4 in range(4):
                        for dc in range(DC):
                            ttr = P.op("tensor", "transpose",
                                       out=ps[dc][:, 0:128],
                                       in_=x4t[p][:, dc * W + s4 * 128:
                                                  dc * W + (s4 + 1) * 128],
                                       identity=eyef[:, :],
                                       wait=[evs[-1], t_ld2, tev])
                            tev = P.op("vector", "tensor_copy",
                                       out=otm[p][:, s4 * 1024 + dc * 128:
                                                  s4 * 1024 +
                                                  (dc + 1) * 128],
                                       in_=ps[dc][:, 0:128],
                                       wait=[ttr] + (prev2 if s4 == 0 and
                                                     dc == 0 else []))
                        P.dma("gpsimd",
                              out=y[b, ti * W + s4 * 128:
                                    ti * W + (s4 + 1) * 128, :],
                              in_=otm[p][:, s4 * 1024:(s4 + 1) * 1024],
                              wait=[tev])
                    trtok = tev
                    tdone[p] = P.now("v", "t") + P.now("dG", "dS")
            P.fence()


def emit_phase_c(nc, P, G):
    _pfx = "C_"
    x2 = G["x2"]; skwT = G["skip_wT"]; inwT = G["in_wT"]; nw_d = G["norm_w"]
    cw_d = G["conv_w"]; cb_d = G["conv_b"]; wd1T = G["wd1T"]
    wd2T = G["wd2T"]; wb2_d = G["wd2_b"]; wbcT = G["wbcT"]
    al_d = G["A_log"]; wD_d = G["W_D"]; owT = G["out_wT"]
    ones_bf = G["ones_bf"]; x3 = G["x3"]
    xs_d = G["xs_scr"]; ssk_d = G["ssk_scr"]; d1_d = G["d1_scr"]
    bc_d = G["bc_scr"]; rows = G["rows_scr"]
    NT = T // 512
    W = 512

    with contextlib.ExitStack() as st:
        en = st.enter_context
        onesb = en(nc.sbuf_tensor(_pfx + "onesb", [128, 2], BF16))
        t_ld0 = P.dma("sync", out=onesb[:, :], in_=ones_bf)

        with contextlib.ExitStack() as s1:
            e1 = s1.enter_context
            wsk = e1(nc.sbuf_tensor(_pfx + "wsk", [128, DC * E1], BF16))
            win = e1(nc.sbuf_tensor(_pfx + "win", [128, DC * E1], BF16))
            nw_sb = e1(nc.sbuf_tensor(_pfx + "nw_sb", [128, DC], F32))
            cw_sb = e1(nc.sbuf_tensor(_pfx + "cw_sb", [128, EC1 * 4], F32))
            cb_sb = e1(nc.sbuf_tensor(_pfx + "cb_sb", [128, EC1], F32))
            wd1_sb = e1(nc.sbuf_tensor(_pfx + "wd1_sb", [128, EC1 * 64], BF16))
            wbc_sb = e1(nc.sbuf_tensor(_pfx + "wbc_sb", [128, EC1 * 32], BF16))

            tnw = []
            for dc in range(DC):
                tnw.append(P.dma("sync", out=nw_sb[:, dc:dc + 1],
                                 in_=nw_d[dc * 128:(dc + 1) * 128]
                                 .rearrange("(p o) -> p o", o=1)))
            tcw = []
            for ec in range(EC1):
                tcw.append(P.dma("sync", out=cw_sb[:, ec * 4:(ec + 1) * 4],
                                 in_=cw_d[ec * 128:(ec + 1) * 128, :]))
                tcw.append(P.dma("sync", out=cb_sb[:, ec:ec + 1],
                                 in_=cb_d[ec * 128:(ec + 1) * 128]
                                 .rearrange("(p o) -> p o", o=1)))
            with contextlib.ExitStack() as sw:
                wtmp = sw.enter_context(
                    nc.sbuf_tensor(_pfx + "wtmpc", [128, E1], F32))
                tw = None
                for w_sb, w_d2 in ((wsk, skwT), (win, inwT)):
                    for c in range(DC):
                        ld = P.dma("sync", out=wtmp[:, :],
                                   in_=w_d2[c * 128:(c + 1) * 128, :],
                                   wait=[tw])
                        tw = P.op("vector", "tensor_scalar_mul",
                                  out=w_sb[:, c * E1:(c + 1) * E1],
                                  in0=wtmp[:, :],
                                  scalar1=nw_sb[:, c:c + 1],
                                  wait=[ld, tw, tnw[c]])
                for c in range(EC1):
                    ld = P.dma("sync", out=wtmp[:, 0:64],
                               in_=wd1T[c * 128:(c + 1) * 128, :], wait=[tw])
                    tw = P.op("vector", "tensor_copy",
                              out=wd1_sb[:, c * 64:(c + 1) * 64],
                              in_=wtmp[:, 0:64], wait=[ld, tw])
                for c in range(EC1):
                    ld = P.dma("sync", out=wtmp[:, 0:32],
                               in_=wbcT[c * 128:(c + 1) * 128, :], wait=[tw])
                    tw = P.op("vector", "tensor_copy",
                              out=wbc_sb[:, c * 32:(c + 1) * 32],
                              in_=wtmp[:, 0:32], wait=[ld, tw])
                P.fence()

            x2t = [e1(nc.sbuf_tensor(_pfx + f"x2tc{i}", [128, DC * W],
                                     F32)) for i in range(2)]
            x2bf1 = e1(nc.sbuf_tensor(_pfx + "x2bfc", [128, DC * W],
                                      BF16))
            x2bf = [x2bf1, x2bf1]
            xsq = e1(nc.sbuf_tensor(_pfx + "xsqc", [128, DC * W], BF16))
            xnf = e1(nc.sbuf_tensor(_pfx + "xnfc", [128, W], F32))
            xn = [e1(nc.sbuf_tensor(_pfx + f"xnc{i}", [128, DC * W], BF16))
                  for i in range(2)]
            nmb = [e1(nc.sbuf_tensor(_pfx + f"nmbc{i}", [128, W], F32))
                   for i in range(2)]
            rbb = [e1(nc.sbuf_tensor(_pfx + f"rbbc{i}", [128, W], F32))
                   for i in range(2)]
            cvin = e1(nc.sbuf_tensor(_pfx + "cvin", [128, EC1 * W], BF16))
            sskb = e1(nc.sbuf_tensor(_pfx + "sskb", [128, W], BF16))
            accv = [e1(nc.sbuf_tensor(_pfx + f"accv{i}", [128, W], F32))
                    for i in range(2)]
            carry = e1(nc.sbuf_tensor(_pfx + "carryc", [128, EC1 * 3], BF16))
            d1t = e1(nc.sbuf_tensor(_pfx + "d1t", [64, W], BF16))
            bct = e1(nc.sbuf_tensor(_pfx + "bct", [32, W], BF16))
            rt2 = [{k: e1(nc.sbuf_tensor(_pfx + f"c1{k}{i}", [1, 512], F32))
                    for k in ("negmu", "var", "r0", "r1", "r2")}
                   for i in range(2)]
            ps_m = [e1(nc.psum_tensor(_pfx + f"ps_mc{i}", [1, 512], F32))
                    for i in range(2)]
            ps_q = [e1(nc.psum_tensor(_pfx + f"ps_qc{i}", [1, 512], F32))
                    for i in range(2)]
            ps_b = [e1(nc.psum_tensor(_pfx + f"ps_bc{i}", [128, 512], F32))
                    for i in range(2)]
            ps_d1 = e1(nc.psum_tensor(_pfx + "ps_d1", [64, 512], F32))
            ps_bc = e1(nc.psum_tensor(_pfx + "ps_bcm", [32, 512], F32))

            tdone = [P.now("v", "s"), P.now("v", "s")]
            pp_free = [None, None]
            cv_free = [None] * EC1
            xs_rd = [None] * EC1
            ssk_prev = None
            ed1 = None
            ebc = None
            tm = None
            tm0 = None
            tsq = None
            tn = None
            for b in range(BPC):
                mz = P.op("vector", "memset", ap=carry[:, :], constant=0.0,
                          wait=P.now("v"))
                for ti in range(NT):
                    it = b * NT + ti
                    p = it % 2
                    prev2 = tdone[p]
                    rt = rt2[p]
                    t0 = ti * W
                    ldx = P.dma("sync",
                                out=x2t[p].ap().rearrange(
                                    "p (c t) -> p c t", c=DC),
                                in_=x2[b, :, t0:t0 + W].rearrange(
                                    "(c p) t -> p c t", p=128),
                                wait=prev2)
                    tcast = P.op("vector", "tensor_copy", out=x2bf[p][:, :],
                                 in_=x2t[p][:, :],
                                 wait=[ldx, tm0, tsq] + prev2)
                    tsq = P.op("scalar", "activation", out=xsq[:, :],
                               in_=x2bf[p][:, :], func=AF.Square,
                               wait=[tcast, tm] if it else [tcast])
                    tm0 = None
                    for c in range(DC):
                        tm0 = P.op("tensor", "matmul", out=ps_m[p][0:1, :],
                                   lhsT=onesb[:, 0:1],
                                   rhs=x2bf[p][:, c * W:(c + 1) * W],
                                   start=(c == 0), stop=(c == DC - 1),
                                   wait=[tcast, t_ld0] + prev2
                                   if c == 0 else [])
                    tm = None
                    for c in range(DC):
                        tm = P.op("tensor", "matmul", out=ps_q[p][0:1, :],
                                  lhsT=onesb[:, 0:1],
                                  rhs=xsq[:, c * W:(c + 1) * W],
                                  start=(c == 0), stop=(c == DC - 1),
                                  wait=[tsq] + prev2 if c == 0 else [])
                    tokm, tokv = _ln_rows(P, rt, ps_m[p], ps_q[p], W, D,
                                          tm0 + tm)
                    tl1 = P.op("scalar", "activation", out=rt["r0"][0:1, :],
                               in_=rt["var"][0:1, :], func=AF.Ln, bias=1e-5,
                               wait=[tokv])
                    tr1 = P.op("scalar", "activation", out=rt["r1"][0:1, :],
                               in_=rt["r0"][0:1, :], func=AF.Exp, scale=-0.5,
                               wait=[tl1])
                    t2a = P.op("vector", "tensor_mul", out=rt["r2"][0:1, :],
                               in0=rt["var"][0:1, :], in1=rt["r1"][0:1, :],
                               wait=[tr1])
                    t2b = P.op("vector", "tensor_mul", out=rt["r2"][0:1, :],
                               in0=rt["r2"][0:1, :], in1=rt["r1"][0:1, :],
                               wait=[t2a])
                    t2c = P.op("scalar", "activation", out=rt["r2"][0:1, :],
                               in_=rt["r2"][0:1, :], func=AF.Ln, bias=1e-5,
                               wait=[t2b])
                    t2d = P.op("scalar", "activation", out=rt["r2"][0:1, :],
                               in_=rt["r2"][0:1, :], func=AF.Exp,
                               scale=-0.5, wait=[t2c])
                    tR5 = P.op("vector", "tensor_mul", out=rt["r0"][0:1, :],
                               in0=rt["r1"][0:1, :], in1=rt["r2"][0:1, :],
                               wait=[t2d, tl1])
                    s0 = P.dma("gpsimd", out=rows[8 + 2 * p:9 + 2 * p, :],
                               in_=rt["negmu"][0:1, :], wait=[tokm] + prev2)
                    s1r = P.dma("gpsimd", out=rows[9 + 2 * p:10 + 2 * p, :],
                                in_=rt["r0"][0:1, :], wait=[tR5])
                    b0 = P.dma("sync", out=nmb[p][:, :],
                               in_=rows[8 + 2 * p:9 + 2 * p, :]
                               .broadcast_to((128, W)), wait=[s0])
                    b1t = P.dma("sync", out=rbb[p][:, :],
                                in_=rows[9 + 2 * p:10 + 2 * p, :]
                                .broadcast_to((128, W)), wait=[s1r])
                    for c in range(DC):
                        t_ = P.op("vector", "tensor_add", out=xnf[:, :],
                                  in0=x2t[p][:, c * W:(c + 1) * W],
                                  in1=nmb[p][:, :],
                                  wait=([b0, ldx] + prev2 if c == 0
                                        else []) + [tn])
                        tn = P.op("vector", "tensor_mul",
                                  out=xn[p][:, c * W:(c + 1) * W],
                                  in0=xnf[:, :], in1=rbb[p][:, :],
                                  wait=[b1t, t_] if c == 0 else [t_])
                    for ec in range(EC1):
                        ps = ps_b[ec % 2]
                        tmm = None
                        for c in range(DC):
                            tmm = P.op(
                                "tensor", "matmul", out=ps[:, :],
                                lhsT=wsk[:, c * E1 + ec * 128:
                                         c * E1 + ec * 128 + 128],
                                rhs=xn[p][:, c * W:(c + 1) * W],
                                start=(c == 0), stop=(c == DC - 1),
                                wait=[tn, pp_free[ec % 2]] if c == 0 else [])
                        ev = P.op("scalar", "activation", out=sskb[:, :],
                                  in_=ps[:, :], func=AF.Silu,
                                  wait=[tmm, pp_free[ec % 2], ssk_prev])
                        sv = P.dma("gpsimd",
                                   out=ssk_d[b, ec * 128:(ec + 1) * 128,
                                             t0:t0 + W],
                                   in_=sskb[:, :], wait=[ev])
                        ssk_prev = sv
                        pp_free[ec % 2] = ev
                    ev_in = [None] * EC1
                    for ec in range(EC1):
                        ps = ps_b[ec % 2]
                        tmm = None
                        for c in range(DC):
                            tmm = P.op(
                                "tensor", "matmul", out=ps[:, :],
                                lhsT=win[:, c * E1 + ec * 128:
                                         c * E1 + ec * 128 + 128],
                                rhs=xn[p][:, c * W:(c + 1) * W],
                                start=(c == 0), stop=(c == DC - 1),
                                wait=[pp_free[ec % 2]] +
                                ([cv_free[ec]] if cv_free[ec] else [])
                                if c == 0 else [])
                        ev = P.op("vector", "tensor_copy",
                                  out=cvin[:, ec * W:(ec + 1) * W],
                                  in_=ps[:, :], lp=True,
                                  wait=[tmm] +
                                  ([xs_rd[ec]] if xs_rd[ec] else []))
                        pp_free[ec % 2] = ev
                        ev_in[ec] = ev
                    xs_toks = []
                    acc_free = [xs_toks, xs_toks]
                    for g in range(0, EC1, 2):
                        toks = {}
                        for k in range(10):
                            for q in range(2):
                                ec = g + q
                                par = ec % 2
                                cv = cvin[:, ec * W:(ec + 1) * W]
                                if k == 0:
                                    toks[(q, 0)] = P.op(
                                        "vector", "tensor_scalar_mul",
                                        out=accv[par][:, :], in0=cv,
                                        scalar1=cw_sb[:, ec * 4 + 3:
                                                      ec * 4 + 4],
                                        wait=[ev_in[ec], tcw[2 * ec],
                                              acc_free[par]])
                                elif k in (1, 3, 5):
                                    kk = (k - 1) // 2
                                    sh = 3 - kk
                                    toks[(q, k)] = P.op(
                                        "vector", "scalar_tensor_tensor",
                                        out=accv[par][:, sh:W],
                                        in0=cvin[:, ec * W:
                                                 (ec + 1) * W - sh],
                                        scalar=cw_sb[:, ec * 4 + kk:
                                                     ec * 4 + kk + 1],
                                        in1=accv[par][:, sh:W],
                                        op0=ALU.mult, op1=ALU.add,
                                        wait=[toks[(q, k - 1)]])
                                elif k in (2, 4, 6):
                                    kk = (k - 2) // 2
                                    sh = 3 - kk
                                    toks[(q, k)] = P.op(
                                        "vector", "scalar_tensor_tensor",
                                        out=accv[par][:, 0:sh],
                                        in0=carry[:, ec * 3 + 3 - sh:
                                                  ec * 3 + 3],
                                        scalar=cw_sb[:, ec * 4 + kk:
                                                     ec * 4 + kk + 1],
                                        in1=accv[par][:, 0:sh],
                                        op0=ALU.mult, op1=ALU.add,
                                        wait=[toks[(q, k - 1)], mz])
                                elif k == 7:
                                    toks[(q, 7)] = P.op(
                                        "vector", "tensor_copy",
                                        out=carry[:, ec * 3:ec * 3 + 3],
                                        in_=cvin[:, (ec + 1) * W - 3:
                                                 (ec + 1) * W],
                                        wait=[toks[(q, 6)]])
                                elif k == 8:
                                    toks[(q, 8)] = P.op(
                                        "scalar", "activation",
                                        out=cvin[:, ec * W:(ec + 1) * W],
                                        in_=accv[par][:, :], func=AF.Silu,
                                        bias=cb_sb[:, ec:ec + 1],
                                        wait=[toks[(q, 6)], toks[(q, 7)],
                                              tcw[2 * ec + 1]])
                                    cv_free[ec] = toks[(q, 8)]
                                else:
                                    toks[(q, 9)] = P.dma(
                                        "gpsimd",
                                        out=xs_d[b,
                                                 ec * 128:(ec + 1) * 128,
                                                 t0:t0 + W],
                                        in_=cvin[:, ec * W:(ec + 1) * W],
                                        wait=[toks[(q, 8)]])
                        for q in range(2):
                            acc_free[(g + q) % 2] = toks[(q, 8)]
                        xs_toks = [toks[(0, 9)], toks[(1, 9)]]
                    tmm = None
                    for ec in range(EC1):
                        tmm = P.op("tensor", "matmul", out=ps_d1[:, :],
                                   lhsT=wd1_sb[:, ec * 64:(ec + 1) * 64],
                                   rhs=cvin[:, ec * W:(ec + 1) * W],
                                   start=(ec == 0), stop=(ec == EC1 - 1),
                                   wait=P.now("v", "s") + [ed1]
                                   if ec == 0 else [])
                    ed1 = P.op("vector", "tensor_copy", out=d1t[:, :],
                               in_=ps_d1[:, :], wait=[tmm], lp=True)
                    sd1 = P.dma("gpsimd", out=d1_d[b, :, t0:t0 + W],
                                in_=d1t[:, :], wait=[ed1])
                    tmm2 = None
                    for ec in range(EC1):
                        tmm2 = P.op("tensor", "matmul", out=ps_bc[:, :],
                                    lhsT=wbc_sb[:, ec * 32:(ec + 1) * 32],
                                    rhs=cvin[:, ec * W:(ec + 1) * W],
                                    start=(ec == 0), stop=(ec == EC1 - 1),
                                    wait=[ebc] if ec == 0 else [])
                    ebc = P.op("vector", "tensor_copy", out=bct[:, :],
                               in_=ps_bc[:, :], wait=[tmm2], lp=True)
                    sbc = P.dma("gpsimd", out=bc_d[b, :, t0:t0 + W],
                                in_=bct[:, :], wait=[ebc])
                    for ec in range(EC1):
                        xs_rd[ec] = [tmm[0], tmm2[0], sd1[0], sbc[0]]
                    tdone[p] = P.now("v", "s", "t") + P.now("dG", "dS")
            P.fence()

        with contextlib.ExitStack() as s2:
            e2 = s2.enter_context
            negA = e2(nc.sbuf_tensor(_pfx + "negA", [128, EC1 * NST], F32))
            wd2_sb = e2(nc.sbuf_tensor(_pfx + "wd2_sb", [64, E1], BF16))
            wb2_sb = e2(nc.sbuf_tensor(_pfx + "wb2_sb", [128, EC1], F32))
            wd_sb = e2(nc.sbuf_tensor(_pfx + "wd_sbc", [128, EC1], F32))
            ow_sb = e2(nc.sbuf_tensor(_pfx + "ow_sb", [128, EC1 * D], BF16))
            with contextlib.ExitStack() as sw:
                wtmp = sw.enter_context(
                    nc.sbuf_tensor(_pfx + "wtmpc2", [128, D], F32))
                tw = None
                for c in range(EC1):
                    ld = P.dma("sync", out=wtmp[:, 0:NST],
                               in_=al_d[c * 128:(c + 1) * 128, :], wait=[tw])
                    t1 = P.op("scalar", "activation",
                              out=negA[:, c * NST:(c + 1) * NST],
                              in_=wtmp[:, 0:NST], func=AF.Exp, wait=[ld])
                    tw = P.op("vector", "tensor_scalar_mul",
                              out=negA[:, c * NST:(c + 1) * NST],
                              in0=negA[:, c * NST:(c + 1) * NST],
                              scalar1=-1.0, wait=[t1])
                ld = P.dma("sync", out=wtmp[0:64, 0:E1 // 2],
                           in_=wd2T[0:64, 0:E1 // 2], wait=[tw])
                tw = P.op("vector", "tensor_copy", out=wd2_sb[:, 0:E1 // 2],
                          in_=wtmp[0:64, 0:E1 // 2], wait=[ld])
                ld = P.dma("sync", out=wtmp[0:64, 0:E1 // 2],
                           in_=wd2T[0:64, E1 // 2:E1], wait=[tw])
                tw = P.op("vector", "tensor_copy", out=wd2_sb[:, E1 // 2:E1],
                          in_=wtmp[0:64, 0:E1 // 2], wait=[ld])
                for c in range(EC1):
                    P.dma("sync", out=wb2_sb[:, c:c + 1],
                          in_=wb2_d[c * 128:(c + 1) * 128]
                          .rearrange("(p o) -> p o", o=1))
                    P.dma("sync", out=wd_sb[:, c:c + 1],
                          in_=wD_d[c * 128:(c + 1) * 128]
                          .rearrange("(p o) -> p o", o=1))
                for c in range(EC1):
                    ld = P.dma("sync", out=wtmp[:, :],
                               in_=owT[c * 128:(c + 1) * 128, :], wait=[tw])
                    tw = P.op("vector", "tensor_copy",
                              out=ow_sb[:, c * D:(c + 1) * D],
                              in_=wtmp[:, :], wait=[ld, tw])
                P.fence()
            tw = P.now("v", "s") + P.now("dS")

            W2 = 256
            NT2 = T // W2
            NW = NST * W2

            x2t = [e2(nc.sbuf_tensor(_pfx + f"x2t{i}", [128, DC * W2], F32))
                   for i in range(2)]
            x2c = [[e2(nc.sbuf_tensor(_pfx + f"x2c{j}_{i}", [128, W2],
                                      BF16)) for i in range(2)]
                   for j in range(2)]
            xqc = [[e2(nc.sbuf_tensor(_pfx + f"xqc{j}_{i}", [128, W2],
                                      BF16)) for i in range(2)]
                   for j in range(2)]
            ln2 = [e2(nc.sbuf_tensor(_pfx + f"ln2_{i}", [128, DC * W2],
                                     F32)) for i in range(2)]
            nmb = [e2(nc.sbuf_tensor(_pfx + f"nmb{i}", [128, W2], F32))
                   for i in range(2)]
            rbb = [e2(nc.sbuf_tensor(_pfx + f"rbb{i}", [128, W2], F32))
                   for i in range(2)]
            xsbf = [e2(nc.sbuf_tensor(_pfx + f"xsbf{i}", [128, EC1 * W2],
                                      BF16)) for i in range(2)]
            sskb = [e2(nc.sbuf_tensor(_pfx + f"sskb{i}", [128, W2], BF16))
                    for i in range(2)]
            btb = [e2(nc.sbuf_tensor(_pfx + f"btb{i}", [128, NW], BF16))
                   for i in range(2)]
            ctb = [e2(nc.sbuf_tensor(_pfx + f"ctb{i}", [128, NW], BF16))
                   for i in range(2)]
            d1t = [e2(nc.sbuf_tensor(_pfx + f"d1t{i}", [64, W2], BF16))
                   for i in range(2)]
            delta = [e2(nc.sbuf_tensor(_pfx + f"dlt{i}", [128, W2], F32))
                     for i in range(2)]
            etm = [e2(nc.sbuf_tensor(_pfx + f"etm{i}", [128, W2], F32))
                   for i in range(2)]
            dxv = [e2(nc.sbuf_tensor(_pfx + f"dxv{i}", [128, W2], BF16))
                   for i in range(2)]
            abarB = [e2(nc.sbuf_tensor(_pfx + f"abB{i}", [128, NW], BF16))
                     for i in range(2)]
            uB = [e2(nc.sbuf_tensor(_pfx + f"uB{i}", [128, NW], BF16))
                  for i in range(2)]
            hB = [e2(nc.sbuf_tensor(_pfx + f"hB{i}", [128, NW], BF16))
                  for i in range(2)]
            yfin = [e2(nc.sbuf_tensor(_pfx + f"yfin{i}", [128, W2], BF16))
                    for i in range(2)]
            fixa = [e2(nc.sbuf_tensor(_pfx + f"fxa{i}", [128, NST], F32))
                    for i in range(2)]
            fix2 = [e2(nc.sbuf_tensor(_pfx + f"fx2{i}", [128, NST], F32))
                    for i in range(2)]
            state = e2(nc.sbuf_tensor(_pfx + "statec", [128, EC1 * NST],
                                      F32))
            zcol = e2(nc.sbuf_tensor(_pfx + "zcol", [128, 1], F32))
            x3o = [[e2(nc.sbuf_tensor(_pfx + f"x3oc{j}_{i}", [128, W2],
                                      F32)) for i in range(2)]
                   for j in range(2)]
            o1 = e2(nc.sbuf_tensor(_pfx + "o1c", [128, W2], F32))
            rt2 = [{k: e2(nc.sbuf_tensor(_pfx + f"c2{k}{i}", [1, W2], F32))
                    for k in ("negmu", "var", "r0", "r1")}
                   for i in range(2)]
            ps_m = [e2(nc.psum_tensor(_pfx + f"ps_mc2{i}", [1, W2], F32))
                    for i in range(2)]
            ps_q = [e2(nc.psum_tensor(_pfx + f"ps_qc2{i}", [1, W2], F32))
                    for i in range(2)]
            ps_d2 = [e2(nc.psum_tensor(_pfx + f"ps_d2{i}", [128, W2], F32))
                     for i in range(2)]
            ps_o = [e2(nc.psum_tensor(_pfx + f"ps_oc{i}", [128, W2], F32))
                    for i in range(2)]

            zc = P.op("vector", "memset", ap=zcol[:, :], constant=0.0,
                      wait=tw)

            def seg(buf, lo=None, hi=None):
                v = buf.ap().rearrange("p (n w) -> p n w", n=NST)
                return v if lo is None else v[:, :, lo:hi]

            def col16(buf, ec=None):
                if ec is None:
                    return buf.ap().rearrange("p (n o) -> p n o", o=1)
                return buf[:, ec * NST:(ec + 1) * NST].rearrange(
                    "p (n o) -> p n o", o=1)

            C = {k: [list(tw), list(tw)] for k in (
                "te1", "td3", "tdx", "scan", "tz", "tyf", "fix2", "ufix",
                "tu", "ty2", "l1")}
            D2 = {k: [list(tw), list(tw)] for k in (
                "evL", "tsqL", "opjL", "tuL", "tzL", "tn", "to1", "psm",
                "psq", "s0", "s1", "b0", "b1", "tact", "opo0", "opo1",
                "x3s0", "x3s1")}
            swb_tok = [list(tw) for _ in range(EC1)]
            swb_hold = [list(tw)]

            for b in range(BPC):
                mz = P.op("vector", "memset", ap=state[:, :], constant=0.0,
                          wait=[C["fix2"][0], C["fix2"][1], swb_hold[0]])
                for ec in range(EC1):
                    swb_tok[ec] = mz
                for ti in range(NT2):
                    it = b * NT2 + ti
                    tp = it % 2
                    t0 = ti * W2
                    rt = rt2[tp]
                    ldx = P.dma("sync",
                                out=x2t[tp].ap().rearrange(
                                    "p (c t) -> p c t", c=DC),
                                in_=x2[b, :, t0:t0 + W2].rearrange(
                                    "(c p) t -> p c t", p=128),
                                wait=[D2["evL"][tp], D2["tsqL"][tp]])
                    lxs = P.dma("sync",
                                out=xsbf[tp].ap().rearrange(
                                    "p (c t) -> p c t", c=EC1),
                                in_=xs_d[b, :, t0:t0 + W2].rearrange(
                                    "(c p) t -> p c t", p=128),
                                wait=[D2["opjL"][tp]])
                    ld1 = P.dma("sync", out=d1t[tp][:, :],
                                in_=d1_d[b, :, t0:t0 + W2],
                                wait=[D2["opjL"][tp]])
                    tbcB = None
                    tbcC = None
                    for n in range(NST):
                        tbcB = P.dma(
                            "scalar",
                            out=btb[tp][:, n * W2:(n + 1) * W2],
                            in_=bc_d[b, n:n + 1, t0:t0 + W2]
                            .broadcast_to((128, W2)),
                            wait=[D2["tuL"][tp]] if n == 0 else [])
                        tbcC = P.dma(
                            "gpsimd",
                            out=ctb[tp][:, n * W2:(n + 1) * W2],
                            in_=bc_d[b, NST + n:NST + n + 1, t0:t0 + W2]
                            .broadcast_to((128, W2)),
                            wait=[D2["tzL"][tp]] if n == 0 else [])
                    tmm = None
                    tmq = None
                    tsq = None
                    tmmb = [D2["opjL"][tp], D2["opjL"][tp]]
                    tmqb = [D2["opjL"][tp], D2["opjL"][tp]]
                    for c in range(DC):
                        tcm = P.op("vector", "tensor_copy",
                                   out=x2c[tp][c % 2][:, :],
                                   in_=x2t[tp][:, c * W2:(c + 1) * W2],
                                   wait=[ldx] + tmmb[c % 2], lp=True)
                        tmm = P.op("tensor", "matmul",
                                   out=ps_m[tp][0:1, :],
                                   lhsT=onesb[:, 0:1],
                                   rhs=x2c[tp][c % 2][:, :],
                                   start=(c == 0), stop=(c == DC - 1),
                                   wait=[tcm, t_ld0] +
                                   ([D2["psm"][tp]] if c == 0 else []))
                        tmmb[c % 2] = tmm
                        tsq = P.op("scalar", "activation",
                                   out=xqc[tp][c % 2][:, :],
                                   in_=x2t[tp][:, c * W2:(c + 1) * W2],
                                   func=AF.Square,
                                   wait=[ldx] + tmqb[c % 2])
                        tmq = P.op("tensor", "matmul",
                                   out=ps_q[tp][0:1, :],
                                   lhsT=onesb[:, 0:1],
                                   rhs=xqc[tp][c % 2][:, :],
                                   start=(c == 0), stop=(c == DC - 1),
                                   wait=[tsq] +
                                   ([D2["psq"][tp]] if c == 0 else []))
                        tmqb[c % 2] = tmq
                    tokm, tokv = _ln_rows(P, rt, ps_m[tp], ps_q[tp], W2, D,
                                          tmm + tmq + D2["s0"][tp] +
                                          D2["s1"][tp] + D2["tact"][tp])
                    tl1 = P.op("scalar", "activation", out=rt["r0"][0:1, :],
                               in_=rt["var"][0:1, :], func=AF.Ln, bias=1e-5,
                               wait=[tokv])
                    tr1 = P.op("scalar", "activation", out=rt["r1"][0:1, :],
                               in_=rt["r0"][0:1, :], func=AF.Exp,
                               scale=-0.5, wait=[tl1, D2["s1"][tp]])
                    s0 = P.dma("gpsimd", out=rows[2 * tp:2 * tp + 1, 0:W2],
                               in_=rt["negmu"][0:1, :],
                               wait=[tokm, D2["b0"][tp]])
                    s1r = P.dma("gpsimd",
                                out=rows[2 * tp + 1:2 * tp + 2, 0:W2],
                                in_=rt["r1"][0:1, :],
                                wait=[tr1, D2["b1"][tp]])
                    b0 = P.dma("sync", out=nmb[tp][:, :],
                               in_=rows[2 * tp:2 * tp + 1, 0:W2]
                               .broadcast_to((128, W2)),
                               wait=[s0, D2["tn"][tp]])
                    b1t = P.dma("sync", out=rbb[tp][:, :],
                                in_=rows[2 * tp + 1:2 * tp + 2, 0:W2]
                                .broadcast_to((128, W2)),
                                wait=[s1r, D2["tn"][tp]])
                    D2["psm"][tp] = tokm
                    D2["psq"][tp] = tokv
                    D2["s0"][tp] = s0
                    D2["s1"][tp] = s1r
                    D2["b0"][tp] = b0
                    D2["b1"][tp] = b1t
                    D2["tsqL"][tp] = tsq
                    D2["tact"][tp] = tr1

                    F = {}
                    tyf_toks = [None] * EC1

                    def emit_front(ec):
                        par = ec % 2
                        td2 = P.op("tensor", "matmul", out=ps_d2[par][:, :],
                                   lhsT=wd2_sb[:, ec * 128:(ec + 1) * 128],
                                   rhs=d1t[tp][:, :], start=True, stop=True,
                                   wait=[ld1, C["te1"][par]])
                        te1 = P.op("scalar", "activation",
                                   out=etm[par][:, :], in_=ps_d2[par][:, :],
                                   func=AF.Exp, bias=wb2_sb[:, ec:ec + 1],
                                   wait=[td2, C["td3"][par]])
                        C["te1"][par] = te1
                        td3 = P.op("scalar", "activation",
                                   out=delta[par][:, :], in_=etm[par][:, :],
                                   func=AF.Ln, bias=1.0,
                                   wait=[te1, C["tdx"][par]])
                        C["td3"][par] = td3
                        tdx = P.op("vector", "tensor_mul",
                                   out=dxv[par][:, :], in0=delta[par][:, :],
                                   in1=xsbf[tp][:, ec * W2:(ec + 1) * W2],
                                   wait=[td3, lxs, C["tu"][par]], lp=True)
                        C["tdx"][par] = tdx
                        tfa = P.op("scalar", "activation",
                                   out=fixa[par][:, :],
                                   in_=negA[:, ec * NST:(ec + 1) * NST],
                                   func=AF.Exp, scale=delta[par][:, 0:1],
                                   wait=[td3, C["fix2"][par]])
                        tf2 = P.op("vector", "tensor_mul",
                                   out=fix2[par][:, :], in0=fixa[par][:, :],
                                   in1=state[:, ec * NST:(ec + 1) * NST],
                                   wait=[tfa, swb_tok[ec], C["ufix"][par]])
                        C["fix2"][par] = tf2
                        tabL = None
                        for n in range(NST):
                            tabL = P.op(
                                "scalar", "activation",
                                out=abarB[par][:, n * W2 + 1:(n + 1) * W2],
                                in_=delta[par][:, 1:W2], func=AF.Exp,
                                scale=negA[:, ec * NST + n:
                                           ec * NST + n + 1],
                                wait=[td3, C["scan"][par]] if n == 0
                                else [])
                        ab0 = P.op("vector", "memset",
                                   ap=seg(abarB[par], 0, 1), constant=0.0,
                                   wait=[C["scan"][par]])
                        tu = P.op("vector", "tensor_mul",
                                  out=seg(uB[par]),
                                  in0=dxv[par].ap().rearrange(
                                      "p (o w) -> p o w", o=1)
                                  .broadcast_to((128, NST, W2)),
                                  in1=seg(btb[tp]),
                                  wait=[tdx, tbcB, C["scan"][par]], lp=True)
                        C["tu"][par] = tu
                        F[ec] = {"tf2": tf2, "tabL": tabL, "ab0": ab0,
                                 "tu": tu}

                    def emit_s1(ec):
                        par = ec % 2
                        f = F[ec]
                        ufix = P.op("vector", "tensor_add",
                                    out=seg(uB[par], 0, 1),
                                    in0=seg(uB[par], 0, 1),
                                    in1=col16(fix2[par]),
                                    wait=[f["tf2"], f["tu"]], lp=True)
                        C["ufix"][par] = ufix
                        th = P.op("vector", "tensor_tensor_scan",
                                  out=hB[par][:, :], data0=abarB[par][:, :],
                                  data1=uB[par][:, :],
                                  initial=zcol[:, 0:1],
                                  op0=ALU.mult, op1=ALU.add,
                                  wait=[f["tabL"], f["ab0"], ufix, zc,
                                        C["ty2"][par], C["l1"][par]],
                                  lp=True)
                        C["scan"][par] = th
                        swb = P.op("scalar", "copy",
                                   out=col16(state, ec),
                                   in_=seg(hB[par], W2 - 1, W2),
                                   wait=[th])
                        swb_tok[ec] = swb
                        swb_hold[0] = swb
                        tz = P.op("vector", "tensor_mul", out=hB[par][:, :],
                                  in0=hB[par][:, :], in1=ctb[tp][:, :],
                                  wait=[th, swb, tbcC, C["ty2"][par],
                                        C["l1"][par]], lp=True)
                        C["tz"][par] = tz
                        l1 = P.op("vector", "tensor_add",
                                  out=hB[par][:, 0:NW // 2],
                                  in0=hB[par][:, 0:NW // 2],
                                  in1=hB[par][:, NW // 2:NW],
                                  wait=[tz, C["ty2"][par]], lp=True)
                        C["l1"][par] = l1
                        F[ec]["th"] = th
                        F[ec]["tz"] = tz
                        F[ec]["l1"] = l1
                        return swb, tz

                    def emit_s2(ec):
                        par = ec % 2
                        f = F.pop(ec)
                        l2 = P.op("vector", "tensor_add",
                                  out=hB[par][:, 0:NW // 4],
                                  in0=hB[par][:, 0:NW // 4],
                                  in1=hB[par][:, NW // 4:NW // 2],
                                  wait=[f["l1"]], lp=True)
                        l3 = P.op("vector", "tensor_add",
                                  out=hB[par][:, 0:2 * W2],
                                  in0=hB[par][:, 0:2 * W2],
                                  in1=hB[par][:, 2 * W2:4 * W2],
                                  wait=[l2], lp=True)
                        l4 = P.op("vector", "tensor_add",
                                  out=hB[par][:, 0:W2],
                                  in0=hB[par][:, 0:W2],
                                  in1=hB[par][:, W2:2 * W2],
                                  wait=[l3], lp=True)
                        ty2 = P.op("vector", "scalar_tensor_tensor",
                                   out=yfin[par][:, :],
                                   in0=xsbf[tp][:, ec * W2:(ec + 1) * W2],
                                   scalar=wd_sb[:, ec:ec + 1],
                                   in1=hB[par][:, 0:W2],
                                   op0=ALU.mult, op1=ALU.add,
                                   wait=[l4, C["tyf"][par]])
                        C["ty2"][par] = ty2
                        lss = P.dma("sync", out=sskb[par][:, :],
                                    in_=ssk_d[b, ec * 128:(ec + 1) * 128,
                                              t0:t0 + W2],
                                    wait=[C["tyf"][par]])
                        tyf = P.op("vector", "tensor_mul",
                                   out=xsbf[tp][:, ec * W2:(ec + 1) * W2],
                                   in0=yfin[par][:, :], in1=sskb[par][:, :],
                                   wait=[ty2, lss], lp=True)
                        C["tyf"][par] = tyf
                        tyf_toks[ec] = tyf

                    for ec in range(EC1):
                        emit_front(ec)
                        if ec >= 1:
                            emit_s1(ec - 1)
                        if ec >= 2:
                            emit_s2(ec - 2)
                    emit_s1(EC1 - 1)
                    emit_s2(EC1 - 2)
                    emit_s2(EC1 - 1)
                    D2["tuL"][tp] = C["tu"][(EC1 - 1) % 2]
                    D2["tzL"][tp] = C["tz"][(EC1 - 1) % 2]

                    tn = None
                    for c in range(DC):
                        t_ = P.op("vector", "tensor_add",
                                  out=ln2[tp][:, c * W2:(c + 1) * W2],
                                  in0=x2t[tp][:, c * W2:(c + 1) * W2],
                                  in1=nmb[tp][:, :],
                                  wait=([b0, ldx, D2["to1"][tp]]
                                        if c == 0 else []) + [tn])
                        tn = P.op("vector", "tensor_mul",
                                  out=ln2[tp][:, c * W2:(c + 1) * W2],
                                  in0=ln2[tp][:, c * W2:(c + 1) * W2],
                                  in1=rbb[tp][:, :],
                                  wait=[b1t, t_] if c == 0 else [t_])
                    D2["tn"][tp] = tn
                    evs = []
                    opj = None
                    for dc in range(DC):
                        ps = ps_o[dc % 2]
                        for ecc in range(EC1):
                            opj = P.op(
                                "tensor", "matmul", out=ps[:, :],
                                lhsT=ow_sb[:, ecc * D + dc * 128:
                                           ecc * D + dc * 128 + 128],
                                rhs=xsbf[tp][:, ecc * W2:(ecc + 1) * W2],
                                start=(ecc == 0), stop=(ecc == EC1 - 1),
                                wait=[tyf_toks[ecc]] +
                                ([D2[f"opo{dc % 2}"][tp]]
                                 if ecc == 0 else []))
                        to1 = P.op("vector", "tensor_add", out=o1[:, :],
                                   in0=ps[:, :],
                                   in1=ln2[tp][:, dc * W2:(dc + 1) * W2],
                                   wait=[opj, tn] +
                                   ([evs[-1]] if evs else []))
                        D2[f"opo{dc % 2}"][tp] = to1
                        ev = P.op("vector", "tensor_add",
                                  out=x3o[tp][dc % 2][:, :], in0=o1[:, :],
                                  in1=x2t[tp][:, dc * W2:(dc + 1) * W2],
                                  wait=[to1, ldx, D2[f"x3s{dc % 2}"][tp]])
                        evs.append(ev)
                        D2[f"x3s{dc % 2}"][tp] = P.dma(
                            "gpsimd",
                            out=x3[b, dc * 128:(dc + 1) * 128, t0:t0 + W2],
                            in_=x3o[tp][dc % 2][:, :], wait=[ev])
                    D2["to1"][tp] = to1
                    D2["evL"][tp] = evs[-1]
                    D2["opjL"][tp] = opj
            P.fence()



def emit_phase_b(nc, P, G):
    _pfx = "B_"
    x1 = G["x1"]; wupT = G["l0f_wupT"]; wdnT = G["l0f_wdnT"]
    ones_bf = G["ones_bf"]; ones_f = G["ones_f"]
    x2 = G["x2"]; h_dram = G["h_scr"]; rows = G["rows_scr"]
    NT = T // 512

    with contextlib.ExitStack() as st:
        en = st.enter_context
        onesb = en(nc.sbuf_tensor(_pfx + "onesb", [128, 2], BF16))
        onesf = en(nc.sbuf_tensor(_pfx + "onesf", [128, 2], F32))
        t_ld0 = P.dma("sync", out=onesb[:, :], in_=ones_bf)
        t_ld1 = P.dma("sync", out=onesf[:, :], in_=ones_f)

        def quantize(en, wq_tile, W_ap, nchunks, width, pfx):
            wtmp = [en(nc.sbuf_tensor(_pfx + f"{pfx}wtmp{i}", [128, width], F32))
                    for i in range(2)]
            accp = en(nc.sbuf_tensor(_pfx + f"{pfx}acc", [128, 2], F32))
            thr = en(nc.sbuf_tensor(_pfx + f"{pfx}thr", [128, 1], F32))
            sg = en(nc.sbuf_tensor(_pfx + f"{pfx}sg", [128, width], BF16))
            ab = en(nc.sbuf_tensor(_pfx + f"{pfx}ab", [128, width], F32))
            mk = en(nc.sbuf_tensor(_pfx + f"{pfx}mk", [128, width], BF16))
            ps_a = en(nc.psum_tensor(_pfx + f"{pfx}psa", [1, 2], F32))
            red = P.op("vector", "memset", ap=accp[:, 0:1], constant=0.0)
            lds = [None, None]
            for c in range(nchunks):
                buf = wtmp[c % 2]
                lds[c % 2] = P.dma("sync", out=buf[:, :], in_=W_ap[c],
                                   wait=[red] if c >= 2 else [])
                red = P.op("vector", "tensor_reduce", out=accp[:, 1:2],
                           in_=buf[:, :], axis=AX.X, op=ALU.add,
                           apply_absolute_value=True,
                           wait=[lds[c % 2], red])
                red = P.op("vector", "tensor_add", out=accp[:, 0:1],
                           in0=accp[:, 0:1], in1=accp[:, 1:2], wait=[red])
            tmm = P.op("tensor", "matmul", out=ps_a[0:1, 0:1],
                       lhsT=accp[:, 0:1], rhs=onesf[:, 0:1],
                       start=True, stop=True, wait=[red, t_ld1])
            trow = P.op("vector", "tensor_scalar_mul", out=accp[0:1, 0:1],
                        in0=ps_a[0:1, 0:1],
                        scalar1=0.5 / (nchunks * 128 * width), wait=[tmm])
            tst = P.dma("gpsimd", out=rows[63:64, 0:1], in_=accp[0:1, 0:1],
                        wait=[trow])
            tbc = P.dma("sync", out=thr[:, :],
                        in_=rows[63:64, 0:1].broadcast_to((128, 1)),
                        wait=[tst])
            t4 = None
            prev = [None, None]
            for c in range(nchunks):
                buf = wtmp[c % 2]
                w = [prev[c % 2]] if prev[c % 2] else []
                ld = P.dma("sync", out=buf[:, :], in_=W_ap[c], wait=w + [tbc])
                t1 = P.op("scalar", "activation", out=sg[:, :], in_=buf[:, :],
                          func=AF.Sign, wait=[ld, t4])
                t2 = P.op("scalar", "activation", out=ab[:, :], in_=buf[:, :],
                          func=AF.Abs, wait=[ld, t4])
                t3 = P.op("vector", "tensor_single_scalar", out=mk[:, :],
                          in_=ab[:, :], scalar=thr[:, 0:1], op=ALU.is_gt,
                          wait=[t2, tbc, t4])
                t4 = P.op("vector", "tensor_mul",
                          out=wq_tile[:, c * width:(c + 1) * width],
                          in0=sg[:, :], in1=mk[:, :], wait=[t1, t3])
                prev[c % 2] = t4
            return P.now("v", "s")

        with contextlib.ExitStack() as s1:
            e1 = s1.enter_context
            wq_up = e1(nc.sbuf_tensor(_pfx + "wq_up", [128, DC * FF], BF16))
            wup_chunks = [wupT[c * 128:(c + 1) * 128, :] for c in range(DC)]
            with contextlib.ExitStack() as sq:
                tq = quantize(sq.enter_context, wq_up, wup_chunks, DC, FF,
                              "q1")
                P.fence()

            x1t = [e1(nc.sbuf_tensor(_pfx + f"x1t{i}", [128, DC * 512],
                                     F32)) for i in range(2)]
            x1bf = [e1(nc.sbuf_tensor(_pfx + f"x1bf{i}", [128, DC * 512],
                                      BF16)) for i in range(2)]
            xsq = e1(nc.sbuf_tensor(_pfx + "xsq", [128, DC * 512], BF16))
            xnf = e1(nc.sbuf_tensor(_pfx + "xnf", [128, 512], F32))
            xn = [e1(nc.sbuf_tensor(_pfx + f"xn{i}", [128, DC * 512],
                                    BF16)) for i in range(2)]
            nmb = [e1(nc.sbuf_tensor(_pfx + f"nmb{i}", [128, 512], F32))
                   for i in range(2)]
            rbb = [e1(nc.sbuf_tensor(_pfx + f"rbb{i}", [128, 512], F32))
                   for i in range(2)]
            hbf = [e1(nc.sbuf_tensor(_pfx + f"hbf{i}", [128, 512], BF16))
                   for i in range(2)]
            hsq = e1(nc.sbuf_tensor(_pfx + "hsq", [128, 512], BF16))
            rt2 = [{k: e1(nc.sbuf_tensor(_pfx + f"b1{k}{i}", [1, 512], F32))
                    for k in ("negmu", "var", "r0", "r1", "r2", "r3")}
                   for i in range(2)]
            ps_m = [e1(nc.psum_tensor(_pfx + f"ps_m{i}", [1, 512], F32))
                    for i in range(2)]
            ps_q = [e1(nc.psum_tensor(_pfx + f"ps_q{i}", [1, 512], F32))
                    for i in range(2)]
            ps_r2 = [e1(nc.psum_tensor(_pfx + f"ps_r2{i}", [1, 512], F32))
                     for i in range(2)]
            ps_big = [e1(nc.psum_tensor(_pfx + f"psb{i}", [128, 512], F32))
                      for i in range(2)]

            W = 512
            tdone = [tq, tq]
            ps_free = [tq, tq]
            hb_free = [tq, tq]
            r2mm = None
            tsq = None
            tn = None
            for b in range(BPC):
                for ti in range(NT):
                    it = b * NT + ti
                    p = it % 2
                    prev2 = tdone[p]
                    rt = rt2[p]
                    src = x1[b, :, ti * W:(ti + 1) * W].rearrange(
                        "(c p) t -> p c t", p=128)
                    ldx = P.dma("sync",
                                out=x1t[p].ap().rearrange(
                                    "p (c t) -> p c t", c=DC),
                                in_=src, wait=prev2)
                    tcast = P.op("vector", "tensor_copy", out=x1bf[p][:, :],
                                 in_=x1t[p][:, :], wait=[ldx] + prev2)
                    tsq = P.op("scalar", "activation", out=xsq[:, :],
                               in_=x1bf[p][:, :], func=AF.Square,
                               wait=[tcast, tm] if it else [tcast])
                    tm0 = None
                    for c in range(DC):
                        tm0 = P.op("tensor", "matmul", out=ps_m[p][0:1, :],
                                   lhsT=onesb[:, 0:1],
                                   rhs=x1bf[p][:, c * W:(c + 1) * W],
                                   start=(c == 0), stop=(c == DC - 1),
                                   wait=[tcast, t_ld0] + prev2
                                   if c == 0 else [])
                    tm = None
                    for c in range(DC):
                        tm = P.op("tensor", "matmul", out=ps_q[p][0:1, :],
                                  lhsT=onesb[:, 0:1],
                                  rhs=xsq[:, c * W:(c + 1) * W],
                                  start=(c == 0), stop=(c == DC - 1),
                                  wait=[tsq] + prev2 if c == 0 else [])
                    tokm, tokv = _ln_rows(P, rt, ps_m[p], ps_q[p], W, D,
                                          tm0 + tm)
                    tl1 = P.op("scalar", "activation", out=rt["r0"][0:1, :],
                               in_=rt["var"][0:1, :], func=AF.Ln, bias=1e-5,
                               wait=[tokv])
                    tr1 = P.op("scalar", "activation", out=rt["r1"][0:1, :],
                               in_=rt["r0"][0:1, :], func=AF.Exp, scale=-0.5,
                               wait=[tl1])
                    tl2 = P.op("scalar", "activation", out=rt["r2"][0:1, :],
                               in_=rt["var"][0:1, :], func=AF.Ln,
                               wait=[tokv, tl1])
                    tr2 = P.op("scalar", "activation", out=rt["r2"][0:1, :],
                               in_=rt["r2"][0:1, :], func=AF.Exp, scale=0.5,
                               wait=[tl2])
                    tz = P.op("vector", "tensor_mul", out=rt["r3"][0:1, :],
                              in0=rt["r2"][0:1, :], in1=rt["r1"][0:1, :],
                              wait=[tr1, tr2])
                    tz = P.op("vector", "tensor_scalar_add",
                              out=rt["r3"][0:1, :], in0=rt["r3"][0:1, :],
                              scalar1=1e-8, wait=[tz])
                    tz = P.op("vector", "reciprocal", out=rt["r2"][0:1, :],
                              in_=rt["r3"][0:1, :], wait=[tz, tr2])
                    tR = P.op("vector", "tensor_mul", out=rt["r0"][0:1, :],
                              in0=rt["r1"][0:1, :], in1=rt["r2"][0:1, :],
                              wait=[tz, tl1])
                    s0 = P.dma("gpsimd", out=rows[8 + 2 * p:9 + 2 * p, :],
                               in_=rt["negmu"][0:1, :], wait=[tokm] + prev2)
                    s1r = P.dma("gpsimd", out=rows[9 + 2 * p:10 + 2 * p, :],
                                in_=rt["r0"][0:1, :], wait=[tR])
                    b0 = P.dma("sync", out=nmb[p][:, :],
                               in_=rows[8 + 2 * p:9 + 2 * p, :]
                               .broadcast_to((128, W)), wait=[s0])
                    b1 = P.dma("sync", out=rbb[p][:, :],
                               in_=rows[9 + 2 * p:10 + 2 * p, :]
                               .broadcast_to((128, W)), wait=[s1r])
                    for c in range(DC):
                        t_ = P.op("vector", "tensor_add", out=xnf[:, :],
                                  in0=x1t[p][:, c * W:(c + 1) * W],
                                  in1=nmb[p][:, :],
                                  wait=([b0, ldx] + prev2 if c == 0
                                        else []) + [tn])
                        tn = P.op("vector", "tensor_mul",
                                  out=xn[p][:, c * W:(c + 1) * W],
                                  in0=xnf[:, :], in1=rbb[p][:, :],
                                  wait=[b1, t_] if c == 0 else [t_])
                    for fc in range(FC):
                        ps = ps_big[fc % 2]
                        tmm = None
                        for c in range(DC):
                            tmm = P.op(
                                "tensor", "matmul", out=ps[:, :],
                                lhsT=wq_up[:, c * FF + fc * 128:
                                           c * FF + fc * 128 + 128],
                                rhs=xn[p][:, c * W:(c + 1) * W],
                                start=(c == 0), stop=(c == DC - 1),
                                wait=[tn, ps_free[fc % 2]] if c == 0 else [])
                        hb = hbf[fc % 2]
                        tg = P.op("scalar", "activation", out=hb[:, :],
                                  in_=ps[:, :], func=AF.Gelu,
                                  wait=[tmm, hb_free[fc % 2]])
                        ps_free[fc % 2] = tg
                        sh = P.dma("gpsimd",
                                   out=h_dram[b, fc * 128:(fc + 1) * 128,
                                              ti * W:(ti + 1) * W],
                                   in_=hb[:, :], wait=[tg])
                        tq2 = P.op("scalar", "activation", out=hsq[:, :],
                                   in_=hb[:, :], func=AF.Square,
                                   wait=[tg, r2mm] if r2mm else [tg])
                        r2mm = P.op("tensor", "matmul",
                                    out=ps_r2[p][0:1, :],
                                    lhsT=onesb[:, 0:1], rhs=hsq[:, :],
                                    start=(fc == 0), stop=(fc == FC - 1),
                                    wait=[tq2] + (prev2 if fc == 0 else []))
                        hb_free[fc % 2] = sh + tq2
                    t5 = P.op("vector", "tensor_scalar_mul",
                              out=rt["r0"][0:1, :], in0=ps_r2[p][0:1, :],
                              scalar1=1.0 / FF, wait=[r2mm, tR])
                    t5 = P.op("scalar", "activation", out=rt["r0"][0:1, :],
                              in_=rt["r0"][0:1, :], func=AF.Ln, wait=[t5])
                    t5 = P.op("scalar", "activation", out=rt["r0"][0:1, :],
                              in_=rt["r0"][0:1, :], func=AF.Exp, scale=0.5,
                              wait=[t5])
                    t5 = P.op("vector", "tensor_scalar_add",
                              out=rt["r0"][0:1, :], in0=rt["r0"][0:1, :],
                              scalar1=1e-8, wait=[t5])
                    t5 = P.op("vector", "reciprocal", out=rt["r1"][0:1, :],
                              in_=rt["r0"][0:1, :], wait=[t5, tR])
                    P.dma("gpsimd", out=rows[20 + it:21 + it, :],
                          in_=rt["r1"][0:1, :], wait=[t5])
                    tdone[p] = P.now("v", "s", "t") + P.now("dG", "dS")
            P.fence()

        with contextlib.ExitStack() as s2:
            e2 = s2.enter_context
            wq_dn = e2(nc.sbuf_tensor(_pfx + "wq_dn", [128, FC * D], BF16))
            wdn_chunks = [wdnT[c * 128:(c + 1) * 128, :] for c in range(FC)]
            with contextlib.ExitStack() as sq:
                tq = quantize(sq.enter_context, wq_dn, wdn_chunks, FC, D,
                              "q2")
                P.fence()

            x1t2 = [e2(nc.sbuf_tensor(_pfx + f"x1t2{i}", [128, DC * 512],
                                      F32)) for i in range(2)]
            x2t = [e2(nc.sbuf_tensor(_pfx + f"x2t{i}", [128, DC * 512],
                                     F32)) for i in range(2)]
            hb2 = [e2(nc.sbuf_tensor(_pfx + f"hb2_{i}", [128, 512], BF16))
                   for i in range(2)]
            r2b = [e2(nc.sbuf_tensor(_pfx + f"r2b{i}", [128, 512], F32))
                   for i in range(2)]
            o1 = e2(nc.sbuf_tensor(_pfx + "o1", [128, 512], F32))
            ps = [e2(nc.psum_tensor(_pfx + f"ps2_{i}", [128, 512], F32))
                  for i in range(8)]

            W = 512
            tdone = [tq, tq]
            evtoks = [tq] * DC
            ldh = [tq, tq]
            mm_par = [tq, tq]
            mm = None
            ev = None
            for b in range(BPC):
                for ti in range(NT):
                    it = b * NT + ti
                    p = it % 2
                    prev2 = tdone[p]
                    src = x1[b, :, ti * W:(ti + 1) * W].rearrange(
                        "(c p) t -> p c t", p=128)
                    ldx = P.dma("sync",
                                out=x1t2[p].ap().rearrange(
                                    "p (c t) -> p c t", c=DC),
                                in_=src, wait=prev2)
                    lbr = P.dma("sync", out=r2b[p][:, :],
                                in_=rows[20 + it:21 + it, :]
                                .broadcast_to((128, W)), wait=prev2)
                    for fc in range(FC):
                        hbuf = hb2[fc % 2]
                        ldh[fc % 2] = P.dma(
                            "sync", out=hbuf[:, :],
                            in_=h_dram[b, fc * 128:(fc + 1) * 128,
                                       ti * W:(ti + 1) * W],
                            wait=[mm_par[fc % 2]])
                        for dc in range(DC):
                            mm = P.op(
                                "tensor", "matmul", out=ps[dc][:, :],
                                lhsT=wq_dn[:, fc * D + dc * 128:
                                           fc * D + dc * 128 + 128],
                                rhs=hbuf[:, :],
                                start=(fc == 0), stop=(fc == FC - 1),
                                wait=([ldh[fc % 2]] +
                                      ([evtoks[dc]] if fc == 0 else []))
                                if dc == 0 or fc == 0 else [])
                            if dc == DC - 1:
                                mm_par[fc % 2] = mm
                    for dc in range(DC):
                        t1 = P.op("vector", "tensor_mul", out=o1[:, :],
                                  in0=ps[dc][:, :], in1=r2b[p][:, :],
                                  wait=([mm, lbr] if dc == 0 else []) +
                                  [ev])
                        ev = P.op("vector", "tensor_add",
                                  out=x2t[p][:, dc * W:(dc + 1) * W],
                                  in0=o1[:, :],
                                  in1=x1t2[p][:, dc * W:(dc + 1) * W],
                                  wait=[t1, ldx] + prev2)
                        evtoks[dc] = t1
                    dst = x2[b, :, ti * W:(ti + 1) * W].rearrange(
                        "(c p) t -> p c t", p=128)
                    P.dma("gpsimd", out=dst,
                          in_=x2t[p].ap().rearrange("p (c t) -> p c t",
                                                    c=DC),
                          wait=[ev])
                    tdone[p] = P.now("v", "t") + P.now("dG", "dS")
            P.fence()





def _declare_drams(nc, mode):
    EI, EO, IN = "ExternalInput", "ExternalOutput", "Internal"
    G = {}

    def add(name, shape, dtp, kind):
        G[name] = nc.dram_tensor(name, shape, dtp, kind=kind).ap()

    NU = T // BLIND
    add("rows_scr", [64, 512], F32, IN)
    if mode in ("a", "fused"):
        add("x", [BPC, T, D], F32, EI)
        for k in ("Wq", "Wk", "Wv", "Wo"):
            add(k, [D, D], F32, EI)
        add("bo", [D], F32, EI)
        add("eye_bf", [128, 128], BF16, EI)
        add("mask256", [128, 256], F32, EI)
        add("mask256f", [128, 256], F32, EI)
        add("bdiag2", [128, 2], BF16, EI)
        add("sel2", [2, 128], BF16, EI)
        add("triu64", [64, 64], F32, EI)
        add("xfm_scr", [BPC, D, T], F32, IN)
        add("qs_scr", [BPC, 768, T], BF16, IN)
        add("kgT_scr", [BPC, T, 768], BF16, IN)
        add("vtg_scr", [BPC, T, 768], BF16, IN)
        add("aloc_scr", [BPC, 256, T], BF16, IN)
        add("aglob_scr", [BPC, 768, T], BF16, IN)
    if mode in ("a", "b", "c", "d", "fused"):
        add("ones_bf", [128, 2], BF16, EI)
    if mode in ("a", "b", "d", "fused"):
        add("ones_f", [128, 2], F32, EI)
    if mode in ("a", "d", "fused"):
        add("eye_f", [128, 128], F32, EI)
    for name, prod, cons in (("x1", "a", "b"), ("x2", "b", "c"),
                             ("x3", "c", "d")):
        kind = (IN if mode == "fused" else
                (EO if mode == prod else (EI if mode == cons else None)))
        if kind is not None:
            add(name, [BPC, D, T], F32, kind)
    if mode in ("b", "fused"):
        add("l0f_wupT", [D, FF], F32, EI)
        add("l0f_wdnT", [FF, D], F32, EI)
    if mode in ("b", "d", "fused"):
        add("h_scr", [BPC, FF, T], BF16, IN)
    if mode in ("c", "fused"):
        add("skip_wT", [D, E1], F32, EI)
        add("in_wT", [D, E1], F32, EI)
        add("norm_w", [D], F32, EI)
        add("conv_w", [E1, 4], F32, EI)
        add("conv_b", [E1], F32, EI)
        add("wd1T", [E1, 64], F32, EI)
        add("wd2T", [64, E1], F32, EI)
        add("wd2_b", [E1], F32, EI)
        add("wbcT", [E1, 32], F32, EI)
        add("A_log", [E1, NST], F32, EI)
        add("W_D", [E1], F32, EI)
        add("out_wT", [E1, D], F32, EI)
        add("xs_scr", [BPC, E1, T], BF16, IN)
        add("ssk_scr", [BPC, E1, T], BF16, IN)
        add("d1_scr", [BPC, 64, T], BF16, IN)
        add("bc_scr", [BPC, 32, T], BF16, IN)
    if mode in ("d", "fused"):
        add("l1f_W1T", [D, FF], F32, EI)
        add("l1f_W2T", [FF, D], F32, EI)
        add("l1f_b1", [FF], F32, EI)
        add("l1f_b2", [D], F32, EI)
        add("y", [BPC, T, D], F32, EO)
    return G


def _build_one(mode):
    nc = bass.Bass("TRN2", target_bir_lowering=False, debug=False)
    P = Prog(nc)
    G = _declare_drams(nc, mode)
    emitters = {"a": [emit_phase_a], "b": [emit_phase_b],
                "c": [emit_phase_c], "d": [emit_phase_d],
                "fused": [emit_phase_a, emit_phase_b, emit_phase_c,
                          emit_phase_d]}[mode]
    with contextlib.ExitStack() as st:
        _setup_cpool(nc, P, st.enter_context)
        for em in emitters:
            em(nc, P, G)
            P.fence()
    P.ops.append(("gpsimd", None, P.all_now(), []))
    return P.build()


def build_phase_a():
    return _build_one("a")


def build_phase_b():
    return _build_one("b")


def build_phase_c():
    return _build_one("c")


def build_phase_d():
    return _build_one("d")


def build_fused():
    return _build_one("fused")


def host_prep(inputs):
    f32 = np.float32
    d = {}
    d["x"] = np.ascontiguousarray(inputs["L0_attn_x"].astype(f32))
    for k in ("Wq", "Wk", "Wv", "Wo"):
        d[k] = np.ascontiguousarray(inputs[f"L0_attn_{k}"])
    d["bo"] = np.ascontiguousarray(inputs["L0_attn_bo"])
    d["l0f_wupT"] = np.ascontiguousarray(inputs["L0_ffn_w_up"].T)
    d["l0f_wdnT"] = np.ascontiguousarray(inputs["L0_ffn_w_down"].T)
    d["norm_w"] = np.ascontiguousarray(inputs["L1_attn_norm_w"])
    d["skip_wT"] = np.ascontiguousarray(inputs["L1_attn_skip_w"].T)
    d["in_wT"] = np.ascontiguousarray(inputs["L1_attn_in_w"].T)
    d["conv_w"] = np.ascontiguousarray(inputs["L1_attn_conv_w"])
    d["conv_b"] = np.ascontiguousarray(inputs["L1_attn_conv_b"])
    d["wd1T"] = np.ascontiguousarray(inputs["L1_attn_wd1"].T)
    d["wd2T"] = np.ascontiguousarray(inputs["L1_attn_wd2"].T)
    d["wd2_b"] = np.ascontiguousarray(inputs["L1_attn_wd2_b"])
    d["wbcT"] = np.ascontiguousarray(
        np.concatenate([inputs["L1_attn_wb"], inputs["L1_attn_wc"]], 0).T)
    d["A_log"] = np.ascontiguousarray(inputs["L1_attn_A_log"])
    d["W_D"] = np.ascontiguousarray(inputs["L1_attn_W_D"])
    d["out_wT"] = np.ascontiguousarray(inputs["L1_attn_out_w"].T)
    d["l1f_W1T"] = np.ascontiguousarray(inputs["L1_ffn_W1"].T)
    d["l1f_b1"] = np.ascontiguousarray(inputs["L1_ffn_b1"])
    d["l1f_W2T"] = np.ascontiguousarray(inputs["L1_ffn_W2"].T)
    d["l1f_b2"] = np.ascontiguousarray(inputs["L1_ffn_b2"])
    d["cpool"] = np.tile(np.array(CPOOL_VALS, f32)[None, :], (128, 1))
    d["ones_bf"] = np.ones((128, 2), BF)
    d["ones_f"] = np.ones((128, 2), f32)
    d["eye_bf"] = np.eye(128, dtype=BF)
    d["eye_f"] = np.eye(128, dtype=f32)
    i = np.arange(128)[:, None]
    j = np.arange(256)[None, :]
    d["mask256"] = ((j < 128) | (i >= (j - 128))).astype(f32)
    d["mask256f"] = ((j >= 128) & (i >= (j - 128))).astype(f32)
    bd = np.zeros((128, 2), BF)
    bd[0:64, 0] = 1
    bd[64:128, 1] = 1
    d["bdiag2"] = bd
    sel = np.zeros((2, 128), BF)
    sel[0, 0:64] = 1
    sel[1, 64:128] = 1
    d["sel2"] = sel
    u = np.arange(64)
    d["triu64"] = (u[:, None] <= u[None, :]).astype(f32)
    return d


def kernel(**inputs):
    d = host_prep(inputs)
    cores = list(range(NCORES))
    nc = build_fused()
    im = []
    for c in cores:
        m = {"x": np.ascontiguousarray(d["x"][BPC * c:BPC * (c + 1)])}
        for k in ("Wq", "Wk", "Wv", "Wo", "bo", "l0f_wupT", "l0f_wdnT",
                  "skip_wT", "in_wT", "norm_w", "conv_w", "conv_b", "wd1T",
                  "wd2T", "wd2_b", "wbcT", "A_log", "W_D", "out_wT",
                  "l1f_W1T", "l1f_b1", "l1f_W2T", "l1f_b2", "ones_bf",
                  "ones_f", "eye_f", "eye_bf", "mask256", "mask256f",
                  "bdiag2", "sel2", "triu64", "cpool"):
            m[k] = d[k]
        im.append(m)
    res = run_bass_kernel_spmd(nc, im, core_ids=cores)
    return np.concatenate([res.results[c]["y"] for c in cores], axis=0)
